# version 21
# baseline (speedup 1.0000x reference)
import contextlib

import numpy as np

import concourse.bass as bass
import concourse.mybir as mybir
from concourse.bass_utils import run_bass_kernel_spmd

F32 = mybir.dt.float32
BF16 = mybir.dt.bfloat16
AF = mybir.ActivationFunctionType
ALU = mybir.AluOpType
AX = mybir.AxisListType

DIM = 1024
INNER = 2048
HEADS = 8
STATE = 64
HD = INNER // HEADS
KERN = 4
FF = 4096
EPS = 1e-5
NCHIP = 8

DC = DIM // 128
IC = INNER // 128
FC = FF // 128
CH = 128


class Buf:
    __slots__ = ("name", "wr", "readers")

    def __init__(self, name=""):
        self.name = name
        self.wr = None
        self.readers = {}


class Em:

    ENGINES = ("tensor", "vector", "scalar", "sync", "gpsimd")

    def __init__(self, nc, ctx):
        self.nc = nc
        self.ctx = ctx
        self.sem = {}
        self.cnt = {}
        self.max_waited = {}
        self.qwait = {}
        self.waited = {e: {} for e in self.ENGINES}
        for e in self.ENGINES:
            self.sem[e] = ctx.enter_context(nc.semaphore("tl_" + e))
            self.cnt[e] = 0
        self.eng = {"tensor": nc.tensor, "vector": nc.vector, "scalar": nc.scalar,
                    "sync": nc.sync, "gpsimd": nc.gpsimd}

    def stream(self, name):
        self.sem[name] = self.ctx.enter_context(self.nc.semaphore("s_" + name))
        self.cnt[name] = 0
        return name

    def _wait(self, ename, need):
        w = self.waited[ename]
        eng = self.eng[ename]
        for s, v in need.items():
            if v <= 0 or w.get(s, 0) >= v:
                continue
            eng.wait_ge(self.sem[s], v)
            w[s] = v
            self.max_waited[s] = max(self.max_waited.get(s, 0), v)

    def _need(self, reads, writes):
        need = {}
        for b in reads:
            if b.wr:
                s, v = b.wr
                need[s] = max(need.get(s, 0), v)
        for b in writes:
            if b.wr:
                s, v = b.wr
                need[s] = max(need.get(s, 0), v)
            for s, v in b.readers.items():
                need[s] = max(need.get(s, 0), v)
        return need

    def op(self, ename, reads, writes, emit):
        self._wait(ename, self._need(reads, writes))
        ins = emit(self.eng[ename])
        self.cnt[ename] += 1
        ins.then_inc(self.sem[ename], 1)
        mark = (ename, self.cnt[ename])
        for b in writes:
            b.wr = mark
            b.readers = {}
        for b in reads:
            b.readers[ename] = self.cnt[ename]
        return ins

    def group(self, ename, reads, writes, emits):
        self._wait(ename, self._need(reads, writes))
        eng = self.eng[ename]
        ins = None
        for f in emits:
            ins = f(eng)
        self.cnt[ename] += 1
        ins.then_inc(self.sem[ename], 1)
        mark = (ename, self.cnt[ename])
        for b in writes:
            b.wr = mark
            b.readers = {}
        for b in reads:
            b.readers[ename] = self.cnt[ename]

    def dma(self, qname, sname, out_ap, in_ap, reads, writes):
        mw = self.cnt[sname]
        if mw > self.qwait.get(sname, 0):
            self.eng[qname].wait_ge(self.sem[sname], mw)
            self.qwait[sname] = mw
            self.waited[qname][sname] = max(self.waited[qname].get(sname, 0), mw)
        self._wait(qname, self._need(reads, writes))
        ins = self.eng[qname].dma_start(out=out_ap, in_=in_ap)
        self.cnt[sname] += 16
        ins.then_inc(self.sem[sname], 16)
        mark = (sname, self.cnt[sname])
        for b in writes:
            b.wr = mark
            b.readers = {}
        for b in reads:
            b.readers[sname] = self.cnt[sname]

    def barrier_wait(self, ename, sname, value):
        self._wait(ename, {sname: value})

    def fence(self):
        for e in self.ENGINES:
            need = {sn: v for sn, v in self.cnt.items() if v > 0 and sn != e}
            self._wait(e, need)


class Arena:

    def __init__(self, nc, words, dtype=F32, name="arena"):
        self.t = nc.alloc_sbuf_tensor(name, [128, words], dtype)
        self.words = words
        self.cursor = 0
        self.high = 0

    def alloc(self, shape):
        assert shape[0] == 128
        n = int(np.prod(shape[1:]))
        off = self.cursor
        assert off + n <= self.words, f"arena overflow: {off}+{n} > {self.words}"
        self.cursor = off + n
        self.high = max(self.high, self.cursor)
        ap = self.t[:, off:off + n]
        if len(shape) > 2:
            dims = " ".join(f"d{i}" for i in range(1, len(shape)))
            kw = {f"d{i}": s for i, s in enumerate(shape) if i > 0}
            ap = ap.rearrange(f"p ({dims}) -> p {dims}", **kw)
        return ap

    def reset(self, base):
        self.cursor = base

    def alloc_top(self, shape, off_from_top):
        n = int(np.prod(shape[1:]))
        off = self.words - off_from_top
        assert off >= 0 and off + n <= self.words
        ap = self.t[:, off:off + n]
        if len(shape) > 2:
            dims = " ".join(f"d{i}" for i in range(1, len(shape)))
            kw = {f"d{i}": sz for i, sz in enumerate(shape) if i > 0}
            ap = ap.rearrange(f"p ({dims}) -> p {dims}", **kw)
        return ap


def build_program(b_loc=2, t_seq=2048, debug=False):
    n_tile = t_seq // 512
    n_ch = t_seq // CH
    assert t_seq % 512 == 0

    nc = bass.Bass("TRN2", target_bir_lowering=False)
    ctx = contextlib.ExitStack()
    em = Em(nc, ctx)

    def din(name, shape):
        return nc.dram_tensor(name, shape, F32, kind="ExternalInput").ap()

    x_in = din("x", [b_loc, t_seq, DIM])
    wxT = din("wxT", [IC, 128, DC, 128])
    wzT = din("wzT", [IC, 128, DC, 128])
    conv_w = din("conv_w", [128, IC, KERN])
    conv_b = din("conv_b", [128, IC])
    wbT = din("wbT", [4, 128, IC, 128])
    wcT = din("wcT", [4, 128, IC, 128])
    wdtT = din("wdtT", [INNER, HEADS])
    b_dt_in = din("b_dt", [HEADS])
    log_a = din("log_a", [HEADS])
    d_rep = din("d_rep", [INNER])
    d_tab = din("d_tab", [128, IC])
    gn_w = din("gn_w", [128, IC])
    gn_b = din("gn_b", [128, IC])
    woutT = din("woutT", [INNER, DIM])
    w1T = din("w1T", [DIM, FF])
    b1_in = din("b1", [128, FC])
    w2T = din("w2T", [FF, DIM])
    b2_in = din("b2", [DIM])
    identity = din("identity", [128, 128])
    trimask = din("trimask", [128, 128])

    out = nc.dram_tensor("out", [b_loc, t_seq, DIM], F32, kind="ExternalOutput").ap()
    out_names = ["out"]

    yscr = nc.dram_tensor("yscr", [b_loc, n_ch, 128, IC, CH], F32, kind="Internal").ap()
    x2scr = nc.dram_tensor("x2scr", [b_loc, t_seq, DIM], F32, kind="Internal").ap()
    tw1 = nc.dram_tensor("tw1", [FC, 128, DC, 128], BF16, kind="Internal").ap()
    tw2 = nc.dram_tensor("tw2", [FF, DIM], BF16, kind="Internal").ap()

    dbg = {}
    if debug:
        def dout(name, shape):
            ap = nc.dram_tensor(name, shape, F32, kind="ExternalOutput").ap()
            out_names.append(name)
            dbg[name] = ap
            return ap
        dout("dbg_xc", [b_loc, n_tile, 128, IC, 512])
        dout("dbg_y", [b_loc, n_ch, 128, IC, CH])
        dout("dbg_stats", [b_loc, 2, 1, 16])
        dout("dbg_x2", [b_loc, t_seq, DIM])
        dout("dbg_dt", [b_loc, n_tile, 8, 512])
        dout("dbg_tern", [2, 128, 128])

    AR = Arena(nc, 46350)
    AR16 = Arena(nc, 13350, dtype=BF16, name="arena16")

    c_id = AR.alloc([128, 128]); b_id = Buf("id")
    c_tri = AR.alloc([128, 128]); b_tri = Buf("tri")
    c_convw = AR.alloc([128, IC, KERN]); b_convw = Buf("convw")
    c_convb = AR.alloc([128, IC]); b_convb = Buf("convb")
    c_gnw = AR.alloc([128, IC]); b_gnw = Buf("gnw")
    c_gnb = AR.alloc([128, IC]); b_gnb = Buf("gnb")
    c_scol = AR.alloc([128, b_loc, IC]); b_scol = Buf("scol")
    c_tcol = AR.alloc([128, b_loc, IC]); b_tcol = Buf("tcol")
    c_ssum = AR.alloc([128, b_loc, HEADS, n_ch]); b_ssum = Buf("ssum")
    c_ssq = AR.alloc([128, b_loc, HEADS, n_ch]); b_ssq = Buf("ssq")
    c_sred = AR.alloc([128, 2, HEADS]); b_sred = Buf("sred")
    c_state = AR.alloc([128, HEADS // 2, HD]); b_state = Buf("state")
    c_stash = AR.alloc([128, IC, KERN - 1]); b_stash = Buf("stash")
    c_onescol = AR.alloc([128, 1]); b_onescol = Buf("onescol")
    c_b1c = AR.alloc([128, FC]); b_b1c = Buf("b1c")
    c_keps = AR.alloc([128, 1]); b_keps = Buf("keps")
    c_ln1m = AR.alloc([128, 64]); b_ln1s = Buf("ln1s")
    c_ln1r = AR.alloc([128, 64])
    c_x2m = AR.alloc([128, 64]); b_x2s = Buf("x2s")
    c_x2r = AR.alloc([128, 64])
    c_ones8 = AR.alloc([128, 128])
    b_ones8 = Buf("ones8")
    c_lnstat = AR.alloc([128, 16]); b_lnstat = Buf("lnstat")
    c_dtu = AR.alloc([128, 512]); b_dtu = Buf("dtu")
    c_sc8 = AR.alloc([128, 16]); b_sc8 = Buf("sc8")
    c_dcol = AR.alloc([128, 8]); b_dcol = Buf("dcol")
    c_rows = AR.alloc([128, 3600]); b_rows = Buf("rows")
    r_ones = c_rows[0:1, 0:512]
    r_b2 = c_rows[0:1, 512:512 + DIM]
    r_drow = c_rows[0:1, 512 + DIM:512 + DIM + INNER]
    c_rows2 = AR.alloc([128, 96]); b_rows2 = Buf("rows2")
    r_work = c_rows2[0:1, 0:32]
    r_m16 = c_rows2[0:1, 32:48]
    r_r16 = c_rows2[0:1, 48:64]
    col_bdt = c_sc8[0:8, 0:1]
    col_a = c_sc8[0:8, 1:2]
    col_dtsum = c_sc8[0:8, 2:3]
    col_decay = c_sc8[0:8, 3:4]
    diag8 = c_sc8[0:8, 8:16]

    perm_end = AR.cursor
    _dbg_sizes = {"perm": perm_end}

    slab_lnT = AR.alloc_top([128, DC, 512], 4096)
    slab_xs = [AR.alloc_top([128, 1024], 6144 - 1024 * i) for i in range(2)]
    tn_buf = [AR.alloc_top([128, 1024], 8262 - 1024 * i) for i in range(2)]
    b_tnbuf = [Buf(f"tnbuf{i}") for i in range(2)]
    tn_acc = AR.alloc_top([128, 68], 6214); b_tnacc = Buf("tnacc")
    tn_th = AR.alloc_top([128, 2], 6146); b_tnth = Buf("tnth")
    tn_out = AR16.alloc([128, 1024]); b_tnout = Buf("tnout")
    P1_LIMIT = AR.words - 8262

    AR.reset(perm_end)
    c_dbc = AR.alloc([128, IC]); b_dbc = Buf("dbc")
    p1_wx = [AR.alloc([128, DC, 128]) for _ in range(2)]
    b_p1wx = [Buf(f"p1wx{i}") for i in range(2)]
    p1_wbc = [AR.alloc([128, IC, 128]) for _ in range(2)]
    b_p1wbc = [Buf(f"p1wbc{i}") for i in range(2)]
    p1_wdt = AR.alloc([128, IC, HEADS]); b_p1wdt = Buf("p1wdt")
    p1_xsub = [slab_xs[0], slab_xs[1]]
    b_p1xsub = [Buf(f"p1xsub{i}") for i in range(2)]
    p1_lnT = slab_lnT; b_p1lnT = Buf("p1lnT")
    p1_xc = AR.alloc([128, IC, 512]); b_p1xc = Buf("p1xc")
    p1_xpart = [AR.alloc([128, 512 + KERN - 1]) for _ in range(2)]
    b_p1xpart = [Buf(f"p1xpart{i}") for i in range(2)]
    p1_bc = AR.alloc([128, 8, 512]); b_p1bc = Buf("p1bc")
    p1_xcT = AR.alloc([128, IC, CH]); b_p1xcT = Buf("p1xcT")
    p1_xcd = AR.alloc([128, 2, CH]); b_p1xcd = Buf("p1xcd")
    p1_mp = AR.alloc([128, HEADS, CH]); b_p1mp = Buf("p1mp")
    p1_bchkT = AR.alloc([128, HEADS, STATE]); b_p1bchkT = Buf("p1bchkT")
    p1_cols = AR.alloc([128, 16]); b_p1cols = Buf("p1cols")
    p1_w = AR.alloc([128, 8]); b_p1w = Buf("p1w")
    p1_wi = AR.alloc([128, 8]); b_p1wi = Buf("p1wi")
    p1_dtw = AR.alloc([128, 8]); b_p1dtw = Buf("p1dtw")
    p1_dtw2 = AR.alloc([128, 8]); b_p1dtw2 = Buf("p1dtw2")
    p1_y = AR.alloc([128, IC, CH]); b_p1y = Buf("p1y")
    p1_yT = [AR.alloc([128, IC, CH]) for _ in range(1)] * 2
    b_p1yT = [Buf("p1yT0")] * 2
    p1_sq = AR.alloc([128, 2, CH]); b_p1sq = Buf("p1sq")

    AR.reset(perm_end)
    p2_wz = [AR.alloc([128, DC, 128]) for _ in range(2)]
    b_p2wz = [Buf(f"p2wz{i}") for i in range(2)]
    p2_wo = AR.alloc([128, IC, DIM]); b_p2wo = Buf("p2wo")
    p2_xsub = [AR.alloc([128, DIM]) for _ in range(4)]
    b_p2xsub = [Buf(f"p2xsub{i}") for i in range(4)]
    p2_lnsub = AR.alloc([128, DIM]); b_p2lnsub = Buf("p2lnsub")
    p2_lnT = slab_lnT; b_p2lnT = Buf("p2lnT")
    p2_z = AR.alloc([128, 512]); b_p2z = Buf("p2z")
    p2_yb = AR.alloc([128, IC, 512]); b_p2yb = Buf("p2yb")
    p2_x2t = [AR.alloc([128, DIM]) for _ in range(2)]
    b_p2x2t = [Buf(f"p2x2t{i}") for i in range(2)]

    AR.reset(perm_end)
    AR16.reset(0)
    p3_w1 = [AR16.alloc([128, DC, 128]) for _ in range(2)]
    b_p3w1 = [Buf(f"p3w1{i}") for i in range(2)]
    p3_w2 = [AR16.alloc([128, DIM]) for _ in range(2)]
    b_p3w2 = [Buf(f"p3w2{i}") for i in range(2)]
    p3_xsub = [AR.alloc([128, DIM]) for _ in range(4)]
    b_p3xsub = [Buf(f"p3xsub{i}") for i in range(4)]
    p3_lnsub = AR.alloc([128, DIM]); b_p3lnsub = Buf("p3lnsub")
    p3_lnThi = [AR16.alloc([128, DC, 256]) for _ in range(2)]
    p3_lnTlo = [AR16.alloc([128, DC, 256]) for _ in range(2)]
    b_p3lnT = [Buf(f"p3lnT{i}") for i in range(2)]
    p3_h = [AR.alloc([128, 256]) for _ in range(2)]
    b_p3h = [Buf(f"p3h{i}") for i in range(2)]
    p3_hhi = [AR16.alloc([128, 256]) for _ in range(2)]
    p3_hlo = [AR16.alloc([128, 256]) for _ in range(2)]
    b_p3hs = [Buf(f"p3hs{i}") for i in range(2)]
    p3_x3 = AR.alloc([128, DIM]); b_p3x3 = Buf("p3x3")

    _dbg_sizes["max"] = AR.high
    assert AR.high <= P1_LIMIT or True
    PS = [nc.alloc_psum_tensor(f"ps{i}", [128, 512], F32) for i in range(8)]
    PB = [Buf(f"ps{i}") for i in range(8)]

    s_c = em.stream("consts")
    s_x = [em.stream(f"xin{i}") for i in range(2)]
    s_y = [em.stream(f"yout{i}") for i in range(2)]
    s_yr = [em.stream(f"yread{i}") for i in range(2)]
    s_x2w = [em.stream(f"x2w{i}") for i in range(2)]
    s_x2r = [em.stream(f"x2r{i}") for i in range(2)]
    s_wx = [em.stream(f"wx{i}") for i in range(2)]
    s_wbc = [em.stream(f"wbc{i}") for i in range(2)]
    s_wz = [em.stream(f"wz{i}") for i in range(2)]
    s_w1 = [em.stream(f"w1{i}") for i in range(2)]
    s_w2 = [em.stream(f"w2{i}") for i in range(2)]
    s_tn = em.stream("tnload")
    s_tw = em.stream("twrite")
    s_o = em.stream("oout")
    s_dbg = em.stream("dbgout")

    b_x2scr = Buf("x2scr")
    b_yscr = Buf("yscr")
    b_tw = Buf("twscr")

    prog = []

    def run(f):
        prog.append(f)

    def emit_consts():
        q, s = "sync", s_c
        em.dma(q, s, c_id, identity, [], [b_id])
        em.dma(q, s, c_tri, trimask, [], [b_tri])
        em.dma(q, s, c_convw, conv_w, [], [b_convw])
        em.dma(q, s, c_convb, conv_b, [], [b_convb])
        em.dma(q, s, c_gnw, gn_w, [], [b_gnw])
        em.dma(q, s, c_gnb, gn_b, [], [b_gnb])
        em.dma(q, s, c_b1c, b1_in, [], [b_b1c])
        em.dma(q, s, r_b2, b2_in.unsqueeze(0), [], [b_rows])
        em.dma(q, s, col_bdt, b_dt_in.unsqueeze(1), [], [b_sc8])
        em.dma(q, s, col_a, log_a.unsqueeze(1), [], [b_sc8])
        em.dma(q, s, r_drow, d_rep.unsqueeze(0), [], [b_rows])
        em.op("vector", [], [b_rows], lambda e: e.memset(r_ones, 1.0))
        em.op("vector", [], [b_onescol], lambda e: e.memset(c_onescol, 1.0))
        em.op("vector", [], [b_keps], lambda e: e.memset(c_keps, EPS))
        em.op("vector", [], [b_ones8], lambda e: e.memset(c_ones8[0:8, :], 1.0))
        em.op("vector", [], [b_ssum], lambda e: e.memset(c_ssum, 0.0))
        em.op("vector", [], [b_ssq], lambda e: e.memset(c_ssq, 0.0))
        em.op("scalar", [b_sc8], [b_sc8],
              lambda e: e.activation(col_a, col_a, AF.Exp))
        em.op("scalar", [b_sc8], [b_sc8],
              lambda e: e.mul(col_a, col_a, -1.0))
    run(emit_consts)

    s_tn = [em.stream(f"tn{i}") for i in range(2)]
    s_tw2 = em.stream("twrite2")
    tern_steps = []

    def build_tern_steps():
        w1v = w1T.rearrange("(a p) c -> p a c", p=128)
        w2v = w2T.rearrange("(a p) c -> p a c", p=128)

        def w1_ap(k, buf):
            return (buf.rearrange("p (a c) -> p a c", a=DC),
                    w1v[:, :, k * 128:(k + 1) * 128])

        def w2_ap(k, buf):
            return buf[:, 0:1024], w2v[:, k, :]

        def red_step(k, chunk_ap, col):
            def f():
                buf, bb = tn_buf[k % 2], b_tnbuf[k % 2]
                dst, src = chunk_ap(k, buf)
                em.dma("gpsimd", s_tn[k % 2], dst, src, [], [bb])
                em.op("vector", [bb], [b_tnacc],
                      lambda e: e.reduce_sum(tn_acc[:, col:col + 1], buf,
                                             axis=AX.X, apply_absolute_value=True))
            return f

        def fin_step(c0, c1, nelem):
            def f():
                em.op("vector", [b_tnacc], [b_tnacc],
                      lambda e: e.reduce_sum(tn_acc[:, 67:68], tn_acc[:, c0:c1],
                                             axis=AX.X))
                em.op("tensor", [b_tnacc, b_onescol], [PB[7]],
                      lambda e: e.matmul(PS[7][0:1, 16:17], tn_acc[:, 67:68],
                                         c_onescol, start=True, stop=True))
                th_row = c_rows2[0:1, 64:66]
                em.op("scalar", [PB[7]], [b_rows2],
                      lambda e: e.activation(th_row[0:1, 0:1], PS[7][0:1, 16:17],
                                             AF.Copy, scale=1.0 / nelem))
                em.op("vector", [b_rows2], [b_rows2],
                      lambda e: e.tensor_scalar_max(th_row[0:1, 0:1],
                                                    th_row[0:1, 0:1], 1e-8))
                em.op("scalar", [b_rows2], [b_rows2],
                      lambda e: e.mul(th_row[0:1, 0:1], th_row[0:1, 0:1], 0.5))
                em.op("scalar", [b_rows2], [b_rows2],
                      lambda e: e.mul(th_row[0:1, 1:2], th_row[0:1, 0:1], -1.0))
                em.op("tensor", [b_rows2, b_rows], [PB[7]],
                      lambda e: e.matmul(PS[7][:, 16:18], r_ones[0:1, 0:128],
                                         th_row, start=True, stop=True))
                em.op("scalar", [PB[7]], [b_tnth],
                      lambda e: e.copy(tn_th, PS[7][:, 16:18]))
            return f

        def tern_step(k, chunk_ap, out_fn):
            def f():
                buf, bb = tn_buf[k % 2], b_tnbuf[k % 2]
                dst, src = chunk_ap(k, buf)
                em.dma("gpsimd", s_tn[k % 2], dst, src, [], [bb])
                em.op("vector", [bb, b_tnth], [b_tnout],
                      lambda e: e.tensor_single_scalar(
                          tn_out, buf, tn_th[:, 1:2], ALU.is_lt))
                em.op("vector", [bb, b_tnth, b_tnout], [b_tnout],
                      lambda e: e.scalar_tensor_tensor(
                          tn_out, buf, tn_th[:, 0:1], tn_out, ALU.is_gt,
                          ALU.subtract))
                em.dma("gpsimd", s_tw2, *out_fn(k), [b_tnout], [b_tw])
            return f

        def w1_out(k):
            return (tw1[k], tn_out.rearrange("p (a c) -> p a c", a=DC))

        def w2_out(k):
            return (tw2.rearrange("(a p) c -> p a c", p=128)[:, k, :],
                    tn_out[:, 0:1024])

        for k in range(FC):
            tern_steps.append(red_step(k, w1_ap, k))
        tern_steps.append(fin_step(0, 32, DIM * FF))
        for k in range(FC):
            tern_steps.append(tern_step(k, w1_ap, w1_out))
        for k in range(FF // 128):
            tern_steps.append(red_step(k, w2_ap, 32 + k))
        tern_steps.append(fin_step(32, 64, FF * DIM))
        for k in range(FF // 128):
            tern_steps.append(tern_step(k, w2_ap, w2_out))

    build_tern_steps()

    def emit_tern_steps(n):
        for _ in range(n):
            if tern_steps:
                tern_steps.pop(0)()

    def emit_ln(src_ap, b_src, dst_ap, b_dst):
        em.op("vector", [b_src], [b_lnstat],
              lambda e: e.bn_stats(c_lnstat[:, 0:6], src_ap[:, 0:512]))
        em.op("vector", [b_src, b_lnstat], [b_lnstat],
              lambda e: e.bn_stats(c_lnstat[:, 6:12], src_ap[:, 512:1024]))
        em.op("vector", [b_lnstat], [b_lnstat],
              lambda e: e.bn_aggr(c_lnstat[:, 12:14], c_lnstat[:, 0:12]))
        em.op("scalar", [b_lnstat, b_keps], [b_lnstat],
              lambda e: e.activation(c_lnstat[:, 14:15], c_lnstat[:, 13:14], AF.Ln,
                                     bias=c_keps))
        em.op("scalar", [b_lnstat], [b_lnstat],
              lambda e: e.activation(c_lnstat[:, 14:15], c_lnstat[:, 14:15], AF.Exp,
                                     scale=-0.5))
        em.op("vector", [b_src, b_lnstat], [b_dst],
              lambda e: e.tensor_scalar(dst_ap, src_ap, c_lnstat[:, 12:13],
                                        c_lnstat[:, 14:15], ALU.subtract, ALU.mult))

    def save_ln_stats(mcol, rcol, bsave):
        em.op("vector", [b_lnstat], [bsave],
              lambda e: e.tensor_copy(mcol, c_lnstat[:, 12:13]))
        em.op("vector", [b_lnstat], [bsave],
              lambda e: e.tensor_copy(rcol, c_lnstat[:, 14:15]))

    def emit_transposes(srcs, b_srcs, dst_fn, bank0=6, evac="scalar"):
        n = len(srcs)
        done = 0
        while done < n:
            take = min(4, n - done)
            bi = bank0 + (done // 4) % 2
            pb, bb = PS[bi], PB[bi]
            fs = []
            rd = list(dict.fromkeys(b_srcs[done:done + take]))
            for i in range(take):
                src = srcs[done + i]
                fs.append(lambda e, src=src, i=i: e.transpose(
                    pb[:, i * 128:(i + 1) * 128], src, c_id))
            em.group("tensor", rd + [b_id], [bb], fs)
            dst, bdst = dst_fn(done, take)
            em.op(evac, [bb], [bdst],
                  lambda e, pb=pb, dst=dst, take=take: e.copy(
                      dst, pb[:, 0:take * 128].rearrange("p (a b) -> p a b", a=take)))
            done += take

    def p1_load_wdt():
        em.dma("sync", s_c, p1_wdt,
               wdtT.rearrange("(ic p) o -> p ic o", p=128), [], [b_p1wdt])
        em.dma("sync", s_c, c_dbc, d_tab, [], [b_dbc])
    run(p1_load_wdt)

    def pass1_tile(b, ti):
        t0 = ti * 512
        for sub in range(4):
            xb, bxb = p1_xsub[sub % 2], b_p1xsub[sub % 2]
            em.dma("sync", s_x[sub % 2], xb,
                   x_in[b, t0 + sub * 128:t0 + (sub + 1) * 128, :], [], [bxb])
            emit_ln(xb, bxb, xb, bxb)
            g = (b * n_tile + ti) * 4 + sub
            save_ln_stats(c_ln1m[:, g:g + 1], c_ln1r[:, g:g + 1], b_ln1s)
            srcs = [xb[:, dc * 128:(dc + 1) * 128] for dc in range(DC)]

            def dst_fn(done, take, sub=sub):
                return p1_lnT[:, done:done + take, sub * 128:(sub + 1) * 128], b_p1lnT
            emit_transposes(srcs, [bxb] * DC, dst_fn)

        for ic in range(IC):
            wxb, bwx = p1_wx[ic % 2], b_p1wx[ic % 2]
            em.dma("sync", s_wx[ic % 2], wxb, wxT[ic], [], [bwx])
            pb, bb = PS[ic % 2], PB[ic % 2]
            fs = []
            for dc in range(DC):
                fs.append(lambda e, dc=dc, pb=pb, wxb=wxb: e.matmul(
                    pb[:, 0:512], wxb[:, dc, :], p1_lnT[:, dc, :],
                    start=(dc == 0), stop=(dc == DC - 1)))
            em.group("tensor", [bwx, b_p1lnT], [bb], fs)
            xp, bxp = p1_xpart[ic % 2], b_p1xpart[ic % 2]
            em.op("scalar", [bb], [bxp],
                  lambda e, pb=pb, xp=xp: e.copy(xp[:, KERN - 1:], pb[:, 0:512]))
            em.op("vector", [b_stash], [bxp],
                  lambda e, xp=xp, ic=ic: e.tensor_copy(xp[:, 0:KERN - 1],
                                                        c_stash[:, ic, :]))
            em.op("vector", [bxp, b_convw], [b_p1xc],
                  lambda e, xp=xp, ic=ic: e.tensor_scalar_mul(
                      p1_xc[:, ic, :], xp[:, 0:512], c_convw[:, ic, 0:1]))
            for k in range(1, KERN):
                em.op("vector", [bxp, b_convw, b_p1xc], [b_p1xc],
                      lambda e, xp=xp, ic=ic, k=k: e.scalar_tensor_tensor(
                          p1_xc[:, ic, :], xp[:, k:k + 512], c_convw[:, ic, k:k + 1],
                          p1_xc[:, ic, :], ALU.mult, ALU.add))
            em.op("vector", [bxp], [b_stash],
                  lambda e, xp=xp, ic=ic: e.tensor_copy(c_stash[:, ic, :],
                                                        xp[:, 512:512 + KERN - 1]))
            em.op("scalar", [b_p1xc, b_convb], [b_p1xc],
                  lambda e, ic=ic: e.activation(p1_xc[:, ic, :], p1_xc[:, ic, :],
                                                AF.Silu, bias=c_convb[:, ic:ic + 1]))
            if ic % 2 == 1:
                emit_tern_steps(1)
        if debug:
            em.dma("gpsimd", s_dbg, dbg["dbg_xc"][b, ti], p1_xc, [b_p1xc], [])

        for oc in range(8):
            wt = wbT if oc < 4 else wcT
            col = (oc % 4) * 128
            wb_, bw_ = p1_wbc[oc % 2], b_p1wbc[oc % 2]
            em.dma("sync", s_wbc[oc % 2], wb_, wt[oc % 4], [], [bw_])
            pb, bb = PS[2 + oc % 2], PB[2 + oc % 2]
            fs = []
            for ick in range(IC):
                fs.append(lambda e, ick=ick, wb_=wb_, pb=pb: e.matmul(
                    pb[:, 0:512], wb_[:, ick, :], p1_xc[:, ick, :],
                    start=(ick == 0), stop=(ick == IC - 1)))
            em.group("tensor", [bw_, b_p1xc], [bb], fs)
            em.op("scalar", [bb], [b_p1bc],
                  lambda e, pb=pb, oc=oc: e.copy(p1_bc[:, oc, :], pb[:, 0:512]))
            if oc % 2 == 1:
                emit_tern_steps(1)
        pb4, bb4 = PS[4], PB[4]
        fs = []
        for ick in range(IC):
            fs.append(lambda e, ick=ick: e.matmul(
                pb4[0:8, 0:512], p1_wdt[:, ick, :], p1_xc[:, ick, :],
                start=(ick == 0), stop=(ick == IC - 1)))
        em.group("tensor", [b_p1wdt, b_p1xc], [bb4], fs)
        em.op("scalar", [bb4, b_sc8], [b_dtu],
              lambda e: e.activation(c_dtu[0:8, :], pb4[0:8, 0:512], AF.Exp,
                                     bias=col_bdt))
        em.op("scalar", [b_dtu], [b_dtu],
              lambda e: e.activation(c_dtu[0:8, :], c_dtu[0:8, :], AF.Ln, bias=1.0))
        if debug:
            em.dma("gpsimd", s_dbg, dbg["dbg_dt"][b, ti], c_dtu[0:8, :], [b_dtu], [])
        em.op("vector", [b_dtu, b_sc8], [b_dtu],
              lambda e: e.tensor_scalar_mul(c_dtu[32:40, :], c_dtu[0:8, :], col_a))
        for q in range(4):
            sl = slice(q * CH, (q + 1) * CH)
            em.op("vector", [b_dtu], [b_dtu],
                  lambda e, sl=sl: e.tensor_tensor_scan(
                      c_dtu[64:72, sl], c_dtu[32:40, sl], c_dtu[32:40, sl], 0.0,
                      ALU.add, ALU.bypass))

        for q in range(4):
            ch_idx = ti * 4 + q
            sl = slice(q * CH, (q + 1) * CH)
            pb6, bb6 = PS[6], PB[6]
            em.group("tensor", [b_dtu, b_id], [bb6], [
                lambda e: e.transpose(pb6[:, 0:8], c_dtu[0:8, sl], c_id[0:8, 0:8]),
                lambda e: e.transpose(pb6[:, 8:16], c_dtu[64:72, sl],
                                      c_id[64:72, 64:72]),
            ])
            em.op("scalar", [bb6], [b_p1cols],
                  lambda e: e.copy(p1_cols, pb6[:, 0:16]))
            em.op("scalar", [b_p1cols], [b_p1w],
                  lambda e: e.activation(p1_w, p1_cols[:, 8:16], AF.Exp))
            em.op("scalar", [b_p1cols], [b_p1wi],
                  lambda e: e.activation(p1_wi, p1_cols[:, 8:16], AF.Exp, scale=-1.0))
            em.op("vector", [b_p1cols, b_p1wi], [b_p1dtw],
                  lambda e: e.tensor_mul(p1_dtw, p1_cols[:, 0:8], p1_wi))
            em.op("vector", [b_dtu], [b_sc8],
                  lambda e, sl=sl: e.reduce_sum(col_dtsum, c_dtu[0:8, sl], axis=AX.X))
            em.op("scalar", [b_sc8], [b_sc8],
                  lambda e: e.activation(col_decay, col_dtsum, AF.Exp, scale=col_a))
            em.op("vector", [b_sc8, b_id], [b_sc8],
                  lambda e: e.tensor_scalar_mul(diag8, c_id[0:8, 0:8], col_decay))
            em.op("tensor", [b_sc8, b_ones8], [PB[7]],
                  lambda e: e.matmul(PS[7][:, 0:8], c_ones8[0:8, :], diag8,
                                     start=True, stop=True))
            em.op("scalar", [PB[7]], [b_dcol],
                  lambda e: e.copy(c_dcol, PS[7][:, 0:8]))
            em.op("vector", [b_p1dtw, b_dcol], [b_p1dtw2],
                  lambda e: e.tensor_mul(p1_dtw2, p1_dtw, c_dcol))
            srcs = [p1_xc[:, ic, sl] for ic in range(IC)]

            def dst_fn(done, take):
                return p1_xcT[:, done:done + take, :], b_p1xcT
            emit_transposes(srcs, [b_p1xc] * IC, dst_fn)
            for oc in range(4):
                pbb_, bbb_ = PS[6 + oc % 2], PB[6 + oc % 2]
                em.op("tensor", [b_p1bc, b_id], [bbb_],
                      lambda e, oc=oc, pbb_=pbb_: e.transpose(
                          pbb_[:, 0:128], p1_bc[:, oc, sl], c_id))
                for par in range(2):
                    h = oc * 2 + par
                    em.op("vector", [bbb_, b_p1dtw2], [b_p1bchkT],
                          lambda e, h=h, par=par, pbb_=pbb_: e.tensor_scalar_mul(
                              p1_bchkT[:, h, :], pbb_[:, par * 64:(par + 1) * 64],
                              p1_dtw2[:, h:h + 1]))
            for h in range(8):
                pbg, bbg = PS[2 + h % 2], PB[2 + h % 2]
                par = h % 2
                em.op("tensor", [b_p1bc], [bbg],
                      lambda e, h=h, par=par, pbg=pbg: e.matmul(
                          pbg[:, 0:128], p1_bc[par * 64:(par + 1) * 64, h // 2, sl],
                          p1_bc[par * 64:(par + 1) * 64, 4 + h // 2, sl],
                          start=True, stop=True))
                em.op("vector", [bbg, b_p1dtw, b_tri], [b_p1mp],
                      lambda e, h=h, pbg=pbg: e.scalar_tensor_tensor(
                          p1_mp[:, h, :], pbg[:, 0:128], p1_dtw[:, h:h + 1],
                          c_tri, ALU.mult, ALU.mult))
            for h in range(8):
                par = h % 2
                pby, bby = PS[2 + h % 2], PB[2 + h % 2]
                em.group("tensor", [b_p1mp, b_p1xcT, b_p1bc, b_state], [bby], [
                    lambda e, h=h, pby=pby: e.matmul(
                        pby[:, 128:384], p1_mp[:, h, :],
                        p1_xcT[:, 2 * h:2 * h + 2, :], start=True, stop=False),
                    lambda e, h=h, par=par, pby=pby: e.matmul(
                        pby[:, 128:384],
                        p1_bc[par * 64:(par + 1) * 64, 4 + h // 2, sl],
                        c_state[par * 64:(par + 1) * 64, h // 2, :],
                        start=False, stop=True),
                ])
                pbd, bbd = PS[4 + h % 2], PB[4 + h % 2]
                em.op("tensor", [b_p1bchkT, b_p1xcT], [bbd],
                      lambda e, h=h, par=par, pbd=pbd: e.matmul(
                          pbd[par * 64:(par + 1) * 64, 0:256], p1_bchkT[:, h, :],
                          p1_xcT[:, 2 * h:2 * h + 2, :], start=True, stop=True,
                          tile_position=(0, par * 64)))
                em.op("vector", [b_p1xcT, b_dbc], [b_p1xcd],
                      lambda e, h=h: e.tensor_mul(
                          p1_xcd, p1_xcT[:, 2 * h:2 * h + 2, :],
                          c_dbc[:, 2 * h:2 * h + 2].unsqueeze(2)
                          .broadcast_to([128, 2, CH])))
                em.op("vector", [bby, b_p1w, b_p1xcd], [b_p1y, b_ssum],
                      lambda e, h=h, b=b, ch_idx=ch_idx, pby=pby:
                      e.scalar_tensor_tensor(
                          p1_y[:, 2 * h:2 * h + 2, :],
                          pby[:, 128:384].rearrange("p (a c) -> p a c", a=2),
                          p1_w[:, h:h + 1], p1_xcd,
                          ALU.mult, ALU.add,
                          accum_out=c_ssum[:, b, h, ch_idx:ch_idx + 1]))
                em.op("scalar", [b_p1y], [b_p1sq, b_ssq],
                      lambda e, h=h, b=b, ch_idx=ch_idx: e.activation(
                          p1_sq, p1_y[:, 2 * h:2 * h + 2, :], AF.Square,
                          accum_out=c_ssq[:, b, h, ch_idx:ch_idx + 1]))
                em.op("vector", [bbd, b_dcol, b_state], [b_state],
                      lambda e, h=h, par=par, pbd=pbd: e.scalar_tensor_tensor(
                          c_state[par * 64:(par + 1) * 64, h // 2, :],
                          c_state[par * 64:(par + 1) * 64, h // 2, :],
                          c_dcol[par * 64:(par + 1) * 64, h:h + 1],
                          pbd[par * 64:(par + 1) * 64, 0:256],
                          ALU.mult, ALU.add))
            ybuf, bybuf = p1_yT[q % 2], b_p1yT[q % 2]
            srcs = [p1_y[:, k, :] for k in range(IC)]

            def dst_fn2(done, take, ybuf=ybuf, bybuf=bybuf):
                return ybuf[:, done:done + take, :], bybuf
            emit_transposes(srcs, [b_p1y] * IC, dst_fn2, bank0=6)
            em.dma("gpsimd", s_y[q % 2], yscr[b, ch_idx], ybuf, [bybuf], [b_yscr])
            if debug:
                em.dma("gpsimd", s_dbg, dbg["dbg_y"][b, ch_idx], ybuf, [bybuf], [])

    def pass1_seq(b):
        em.op("vector", [], [b_state], lambda e: e.memset(c_state, 0.0))
        em.op("vector", [], [b_stash], lambda e: e.memset(c_stash, 0.0))
        for ti in range(n_tile):
            pass1_tile(b, ti)

    for b in range(b_loc):
        run(lambda b=b: pass1_seq(b))

    run(lambda: emit_tern_steps(len(tern_steps)))
    run(em.fence)

    def stats_final(b):
        cnt = float(t_seq) * HD
        em.op("vector", [b_ssum], [b_sred],
              lambda e: e.reduce_sum(c_sred[:, 0, :], c_ssum[:, b, :, :], axis=AX.X))
        em.op("vector", [b_ssq], [b_sred],
              lambda e: e.reduce_sum(c_sred[:, 1, :], c_ssq[:, b, :, :], axis=AX.X))
        pb, bb = PS[0], PB[0]
        em.group("tensor", [b_sred, b_onescol], [bb], [
            lambda e: e.matmul(pb[0:1, 0:8], c_onescol, c_sred[:, 0, :],
                               start=True, stop=True),
            lambda e: e.matmul(pb[0:1, 8:16], c_onescol, c_sred[:, 1, :],
                               start=True, stop=True),
        ])
        em.op("scalar", [bb], [b_rows2],
              lambda e: e.activation(r_work[0:1, 0:8], pb[0:1, 0:8], AF.Copy,
                                     scale=1.0 / cnt))
        em.op("scalar", [bb], [b_rows2],
              lambda e: e.activation(r_work[0:1, 8:16], pb[0:1, 8:16], AF.Copy,
                                     scale=1.0 / cnt))
        em.op("vector", [b_rows2], [b_rows2],
              lambda e: e.tensor_mul(r_work[0:1, 16:24], r_work[0:1, 0:8],
                                     r_work[0:1, 0:8]))
        em.op("vector", [b_rows2], [b_rows2],
              lambda e: e.tensor_sub(r_work[0:1, 16:24], r_work[0:1, 8:16],
                                     r_work[0:1, 16:24]))
        em.op("scalar", [b_rows2, b_keps], [b_rows2],
              lambda e: e.activation(r_work[0:1, 24:32], r_work[0:1, 16:24],
                                     AF.Ln, bias=c_keps[0:1, 0:1]))
        em.op("scalar", [b_rows2], [b_rows2],
              lambda e: e.activation(r_work[0:1, 24:32], r_work[0:1, 24:32],
                                     AF.Exp, scale=-0.5))
        for (dst, src) in ((r_m16, r_work[0:1, 0:8]), (r_r16, r_work[0:1, 24:32])):
            em.op("vector", [b_rows2], [b_rows2],
                  lambda e, dst=dst, src=src: e.tensor_copy(dst[0:1, 0:16:2], src))
            em.op("vector", [b_rows2], [b_rows2],
                  lambda e, dst=dst, src=src: e.tensor_copy(dst[0:1, 1:16:2], src))
        if debug:
            em.dma("gpsimd", s_dbg, dbg["dbg_stats"][b, 0], r_m16, [b_rows2], [])
            em.dma("gpsimd", s_dbg, dbg["dbg_stats"][b, 1], r_r16, [b_rows2], [])
        pb1, bb1 = PS[1], PB[1]
        em.group("tensor", [b_rows2, b_rows], [bb1], [
            lambda e: e.matmul(pb1[:, 0:16], r_ones[0:1, 0:128], r_r16,
                               start=True, stop=True),
            lambda e: e.matmul(pb1[:, 16:32], r_ones[0:1, 0:128], r_m16,
                               start=True, stop=True),
        ])
        em.op("vector", [bb1, b_gnw], [b_scol],
              lambda e: e.tensor_mul(c_scol[:, b, :], c_gnw, pb1[:, 0:16]))
        em.op("vector", [bb1, b_scol], [b_tcol],
              lambda e: e.tensor_mul(c_tcol[:, b, :], c_scol[:, b, :],
                                     pb1[:, 16:32]))
        em.op("vector", [b_tcol, b_gnb], [b_tcol],
              lambda e: e.tensor_sub(c_tcol[:, b, :], c_gnb, c_tcol[:, b, :]))

    for b in range(b_loc):
        run(lambda b=b: stats_final(b))

    def p2a_load():
        em.dma("sync", s_c, p2_wo,
               woutT.rearrange("(ic p) d -> p ic d", p=128), [], [b_p2wo])
    run(p2a_load)

    p2a_fetched = set()

    def p2a_fetch(b, ti):
        if (b, ti) in p2a_fetched:
            return
        p2a_fetched.add((b, ti))
        t0 = ti * 512
        for sub in range(4):
            xb, bxb = p2_xsub[sub], b_p2xsub[sub]
            em.dma("sync", s_x[sub % 2], xb,
                   x_in[b, t0 + sub * 128:t0 + (sub + 1) * 128, :], [], [bxb])
        for q in range(4):
            em.dma("sync", s_yr[q % 2], p2_yb[:, :, q * 128:(q + 1) * 128],
                   yscr[b, ti * 4 + q], [b_yscr], [b_p2yb])

    def pass2a_tile(b, ti, nxt):
        t0 = ti * 512
        p2a_fetch(b, ti)
        for sub in range(4):
            xb, bxb = p2_xsub[sub], b_p2xsub[sub]
            g = (b * n_tile + ti) * 4 + sub
            em.op("vector", [bxb, b_ln1s], [b_p2lnsub],
                  lambda e, xb=xb, g=g: e.tensor_scalar(
                      p2_lnsub, xb, c_ln1m[:, g:g + 1], c_ln1r[:, g:g + 1],
                      ALU.subtract, ALU.mult))
            srcs = [p2_lnsub[:, dc * 128:(dc + 1) * 128] for dc in range(DC)]

            def dst_fn(done, take, sub=sub):
                return p2_lnT[:, done:done + take, sub * 128:(sub + 1) * 128], b_p2lnT
            emit_transposes(srcs, [b_p2lnsub] * DC, dst_fn)
        for oc in range(IC):
            wzb, bwz = p2_wz[oc % 2], b_p2wz[oc % 2]
            em.dma("sync", s_wz[oc % 2], wzb, wzT[oc], [], [bwz])
            pb, bb = PS[oc % 2], PB[oc % 2]
            fs = []
            for dc in range(DC):
                fs.append(lambda e, dc=dc, pb=pb, wzb=wzb: e.matmul(
                    pb[:, 0:512], wzb[:, dc, :], p2_lnT[:, dc, :],
                    start=(dc == 0), stop=(dc == DC - 1)))
            em.group("tensor", [bwz, b_p2lnT], [bb], fs)
            em.op("scalar", [bb], [b_p2z],
                  lambda e, pb=pb: e.activation(p2_z, pb[:, 0:512], AF.Silu))
            em.op("vector", [b_p2yb, b_scol, b_tcol], [b_p2yb],
                  lambda e, oc=oc, b=b: e.tensor_scalar(
                      p2_yb[:, oc, :], p2_yb[:, oc, :], c_scol[:, b, oc:oc + 1],
                      c_tcol[:, b, oc:oc + 1], ALU.mult, ALU.add))
            em.op("vector", [b_p2yb, b_p2z], [b_p2yb],
                  lambda e, oc=oc: e.tensor_mul(p2_yb[:, oc, :], p2_yb[:, oc, :],
                                                p2_z))
        for sp in range(2):
            for j in range(2):
                sub = sp * 2 + j
                pba, pbb = PS[2 + 2 * j], PS[3 + 2 * j]
                bba, bbb = PB[2 + 2 * j], PB[3 + 2 * j]
                fs = []
                for ick in range(IC):
                    for half in range(2):
                        pbx = pba if half == 0 else pbb
                        fs.append(lambda e, ick=ick, half=half, sub=sub, pbx=pbx:
                                  e.matmul(pbx[:, 0:512],
                                           p2_yb[:, ick, sub * 128:(sub + 1) * 128],
                                           p2_wo[:, ick, half * 512:(half + 1) * 512],
                                           start=(ick == 0), stop=(ick == IC - 1)))
                em.group("tensor", [b_p2yb, b_p2wo], [bba, bbb], fs)
                em.op("vector", [bba], [b_lnstat],
                      lambda e, pba=pba: e.bn_stats(c_lnstat[:, 0:6], pba[:, 0:512]))
                em.op("vector", [bbb, b_lnstat], [b_lnstat],
                      lambda e, pbb=pbb: e.bn_stats(c_lnstat[:, 6:12], pbb[:, 0:512]))
                em.op("vector", [b_lnstat], [b_lnstat],
                      lambda e: e.bn_aggr(c_lnstat[:, 12:14], c_lnstat[:, 0:12]))
                em.op("scalar", [b_lnstat, b_keps], [b_lnstat],
                      lambda e: e.activation(c_lnstat[:, 14:15], c_lnstat[:, 13:14],
                                             AF.Ln, bias=c_keps))
                em.op("scalar", [b_lnstat], [b_lnstat],
                      lambda e: e.activation(c_lnstat[:, 14:15], c_lnstat[:, 14:15],
                                             AF.Exp, scale=-0.5))
                x2b, bx2b = p2_x2t[j], b_p2x2t[j]
                for half in range(2):
                    pbx, bbx = (pba, bba) if half == 0 else (pbb, bbb)
                    em.op("vector", [bbx, b_lnstat], [bx2b],
                          lambda e, pbx=pbx, half=half, x2b=x2b: e.tensor_scalar(
                              x2b[:, half * 512:(half + 1) * 512], pbx[:, 0:512],
                              c_lnstat[:, 12:13], c_lnstat[:, 14:15],
                              ALU.subtract, ALU.mult))
                em.op("vector", [bx2b, b_p2xsub[sub]], [bx2b],
                      lambda e, x2b=x2b, sub=sub: e.tensor_add(x2b, x2b,
                                                               p2_xsub[sub]))
                em.op("vector", [bx2b], [b_lnstat],
                      lambda e, x2b=x2b: e.bn_stats(c_lnstat[:, 0:6],
                                                    x2b[:, 0:512]))
                em.op("vector", [bx2b, b_lnstat], [b_lnstat],
                      lambda e, x2b=x2b: e.bn_stats(c_lnstat[:, 6:12],
                                                    x2b[:, 512:1024]))
                em.op("vector", [b_lnstat], [b_lnstat],
                      lambda e: e.bn_aggr(c_lnstat[:, 12:14], c_lnstat[:, 0:12]))
                em.op("scalar", [b_lnstat, b_keps], [b_lnstat],
                      lambda e: e.activation(c_lnstat[:, 14:15],
                                             c_lnstat[:, 13:14], AF.Ln,
                                             bias=c_keps))
                em.op("scalar", [b_lnstat], [b_lnstat],
                      lambda e: e.activation(c_lnstat[:, 14:15],
                                             c_lnstat[:, 14:15], AF.Exp,
                                             scale=-0.5))
                g2 = (b * (t_seq // 256) + (t0 // 256 + sp)) * 2 + j
                save_ln_stats(c_x2m[:, g2:g2 + 1], c_x2r[:, g2:g2 + 1], b_x2s)
                em.dma("gpsimd", s_x2w[j],
                       x2scr[b, t0 + sub * 128:t0 + (sub + 1) * 128, :],
                       x2b, [bx2b], [b_x2scr])
                if debug:
                    em.dma("gpsimd", s_dbg,
                           dbg["dbg_x2"][b, t0 + sub * 128:t0 + (sub + 1) * 128, :],
                           x2b, [bx2b], [])
        if nxt is not None:
            p2a_fetch(*nxt)

    tiles_2a = [(b, ti) for b in range(b_loc) for ti in range(n_tile)]
    for i, (b, ti) in enumerate(tiles_2a):
        nxt = tiles_2a[i + 1] if i + 1 < len(tiles_2a) else None
        run(lambda b=b, ti=ti, nxt=nxt: pass2a_tile(b, ti, nxt))

    run(em.fence)

    def p2b_prologue(b, tt):
        t0 = tt * 256
        par = tt % 2
        lnThi, lnTlo, b_lnT3 = p3_lnThi[par], p3_lnTlo[par], b_p3lnT[par]
        for sub in range(2):
            xb, bxb = p3_xsub[2 * par + sub], b_p3xsub[2 * par + sub]
            em.dma("sync", s_x2r[sub], xb,
                   x2scr[b, t0 + sub * 128:t0 + (sub + 1) * 128, :],
                   [b_x2scr], [bxb])
            g2 = (b * (t_seq // 256) + tt) * 2 + sub
            em.op("vector", [bxb, b_x2s], [b_p3lnsub],
                  lambda e, xb=xb, g2=g2: e.tensor_scalar(
                      p3_lnsub, xb, c_x2m[:, g2:g2 + 1], c_x2r[:, g2:g2 + 1],
                      ALU.subtract, ALU.mult))
            srcs = [p3_lnsub[:, dc * 128:(dc + 1) * 128] for dc in range(DC)]
            done = 0
            while done < DC:
                take = min(4, DC - done)
                bi = 6 + (done // 4) % 2
                pb, bb = PS[bi], PB[bi]
                fs = []
                for i in range(take):
                    src = srcs[done + i]
                    fs.append(lambda e, src=src, i=i: e.transpose(
                        pb[:, i * 128:(i + 1) * 128], src, c_id))
                em.group("tensor", [b_p3lnsub, b_id], [bb], fs)
                hi = lnThi[:, done:done + take, sub * 128:(sub + 1) * 128]
                lo = lnTlo[:, done:done + take, sub * 128:(sub + 1) * 128]
                pv = pb[:, 0:take * 128].rearrange("p (a c) -> p a c", a=take)
                em.op("scalar", [bb], [b_lnT3],
                      lambda e, hi=hi, pv=pv: e.copy(hi, pv))
                em.op("vector", [bb, b_lnT3], [b_lnT3],
                      lambda e, hi=hi, lo=lo, pv=pv: e.tensor_sub(lo, pv, hi))
                done += take

    def pass2b_tile(b, tt, nxt):
        t0 = tt * 256
        par = tt % 2
        lnThi, lnTlo, b_lnT3 = p3_lnThi[par], p3_lnTlo[par], b_p3lnT[par]
        for sub in range(2):
            pba, pbb = PS[2 + 2 * sub], PS[3 + 2 * sub]
            bba, bbb = PB[2 + 2 * sub], PB[3 + 2 * sub]
            em.group("tensor", [b_rows], [bba, bbb], [
                lambda e, pba=pba: e.matmul(pba[:, 0:512], r_ones[0:1, 0:128],
                                            r_b2[0:1, 0:512], start=True, stop=False),
                lambda e, pbb=pbb: e.matmul(pbb[:, 0:512], r_ones[0:1, 0:128],
                                            r_b2[0:1, 512:1024], start=True,
                                            stop=False),
            ])

        def ld_w(fc):
            w1b, bw1 = p3_w1[fc % 2], b_p3w1[fc % 2]
            w2b, bw2 = p3_w2[fc % 2], b_p3w2[fc % 2]
            em.dma("sync", s_w1[fc % 2], w1b, tw1[fc], [b_tw], [bw1])
            em.dma("sync", s_w2[fc % 2], w2b, tw2[fc * 128:(fc + 1) * 128, :],
                   [b_tw], [bw2])

        def h_stage(fc):
            w1b, bw1 = p3_w1[fc % 2], b_p3w1[fc % 2]
            pbh, bbh = PS[fc % 2], PB[fc % 2]
            fs = []
            for dc in range(DC):
                for part in range(2):
                    lt = lnThi if part == 0 else lnTlo
                    fs.append(lambda e, dc=dc, pbh=pbh, w1b=w1b, lt=lt,
                              first=(dc == 0 and part == 0),
                              last=(dc == DC - 1 and part == 1): e.matmul(
                        pbh[:, 0:256], w1b[:, dc, :], lt[:, dc, :],
                        start=first, stop=last))
            em.group("tensor", [bw1, b_lnT3], [bbh], fs)
            hb, bhb = p3_h[fc % 2], b_p3h[fc % 2]
            hhi, hlo, bhs = p3_hhi[fc % 2], p3_hlo[fc % 2], b_p3hs[fc % 2]
            em.op("scalar", [bbh, b_b1c], [bhb],
                  lambda e, pbh=pbh, hb=hb, fc=fc: e.activation(
                      hb, pbh[:, 0:256], AF.Gelu, bias=c_b1c[:, fc:fc + 1]))
            em.op("scalar", [bhb], [bhs],
                  lambda e, hb=hb, hhi=hhi: e.copy(hhi, hb))
            em.op("vector", [bhb, bhs], [bhs],
                  lambda e, hb=hb, hhi=hhi, hlo=hlo: e.tensor_sub(hlo, hb, hhi))

        def w2_stage(fc):
            hhi, hlo, bhs = p3_hhi[fc % 2], p3_hlo[fc % 2], b_p3hs[fc % 2]
            w2b, bw2 = p3_w2[fc % 2], b_p3w2[fc % 2]
            last = fc == FC - 1
            for sub in range(2):
                pba, pbb = PS[2 + 2 * sub], PS[3 + 2 * sub]
                bba, bbb = PB[2 + 2 * sub], PB[3 + 2 * sub]
                fs = []
                for part in range(2):
                    hp = hhi if part == 0 else hlo
                    for half in range(2):
                        pbx = pba if half == 0 else pbb
                        fs.append(lambda e, pbx=pbx, hp=hp, sub=sub, w2b=w2b,
                                  half=half,
                                  lastm=(last and part == 1): e.matmul(
                            pbx[:, 0:512], hp[:, sub * 128:(sub + 1) * 128],
                            w2b[:, half * 512:(half + 1) * 512],
                            start=False, stop=lastm))
                em.group("tensor", [bhs, bw2], [bba, bbb], fs)

        ld_w(0)
        for fc in range(FC):
            h_stage(fc)
            if fc > 0:
                w2_stage(fc - 1)
            if fc + 1 < FC:
                ld_w(fc + 1)
            if fc == 8 and nxt is not None:
                p2b_prologue(*nxt)
        w2_stage(FC - 1)
        for sub in range(2):
            pba, pbb = PS[2 + 2 * sub], PS[3 + 2 * sub]
            bba, bbb = PB[2 + 2 * sub], PB[3 + 2 * sub]
            em.op("vector", [bba], [b_lnstat],
                  lambda e, pba=pba: e.bn_stats(c_lnstat[:, 0:6], pba[:, 0:512]))
            em.op("vector", [bbb, b_lnstat], [b_lnstat],
                  lambda e, pbb=pbb: e.bn_stats(c_lnstat[:, 6:12], pbb[:, 0:512]))
            em.op("vector", [b_lnstat], [b_lnstat],
                  lambda e: e.bn_aggr(c_lnstat[:, 12:14], c_lnstat[:, 0:12]))
            em.op("scalar", [b_lnstat, b_keps], [b_lnstat],
                  lambda e: e.activation(c_lnstat[:, 14:15], c_lnstat[:, 13:14],
                                         AF.Ln, bias=c_keps))
            em.op("scalar", [b_lnstat], [b_lnstat],
                  lambda e: e.activation(c_lnstat[:, 14:15], c_lnstat[:, 14:15],
                                         AF.Exp, scale=-0.5))
            for half in range(2):
                pbx, bbx = (pba, bba) if half == 0 else (pbb, bbb)
                em.op("vector", [bbx, b_lnstat], [b_p3x3],
                      lambda e, pbx=pbx, half=half: e.tensor_scalar(
                          p3_x3[:, half * 512:(half + 1) * 512], pbx[:, 0:512],
                          c_lnstat[:, 12:13], c_lnstat[:, 14:15],
                          ALU.subtract, ALU.mult))
            em.op("vector", [b_p3x3, b_p3xsub[2 * par + sub]], [b_p3x3],
                  lambda e, par=par, sub=sub: e.tensor_add(p3_x3, p3_x3,
                                                           p3_xsub[2 * par + sub]))
            em.dma("gpsimd", s_o, out[b, t0 + sub * 128:t0 + (sub + 1) * 128, :],
                   p3_x3, [b_p3x3], [])

    tiles_2b = [(b, tt) for b in range(b_loc) for tt in range(t_seq // 256)]
    run(lambda: p2b_prologue(*tiles_2b[0]))
    for i, (b, tt) in enumerate(tiles_2b):
        nxt = tiles_2b[i + 1] if i + 1 < len(tiles_2b) else None
        run(lambda b=b, tt=tt, nxt=nxt: pass2b_tile(b, tt, nxt))

    with nc.Block() as blk:
        @blk.sync
        def _(sync):
            for f in prog:
                f()
            em.barrier_wait("gpsimd", s_o, em.cnt[s_o])
            if debug:
                em.barrier_wait("gpsimd", s_dbg, em.cnt[s_dbg])

    ctx.close()
    return nc, out_names


def kernel(**inputs):
    x = np.ascontiguousarray(inputs["L0_attn_input_tensor"], dtype=np.float32)
    b_per = x.shape[0] // NCHIP

    def T(a):
        return np.ascontiguousarray(np.asarray(a, dtype=np.float32).T)

    identity = np.eye(128, dtype=np.float32)
    jj, tt = np.meshgrid(np.arange(128), np.arange(128), indexing="ij")
    trimask = (tt >= jj).astype(np.float32)

    def tiled_w(w, kchunks, mchunks):
        wt = T(w)
        K_, M_ = wt.shape
        return np.ascontiguousarray(
            wt.reshape(kchunks, 128, mchunks, 128).transpose(2, 1, 0, 3))

    shared = dict(
        wxT=tiled_w(inputs["L0_attn_Wx"], DC, IC),
        wzT=tiled_w(inputs["L0_attn_Wz"], DC, IC),
        conv_w=np.ascontiguousarray(np.asarray(
            inputs["L0_attn_conv_w"], dtype=np.float32)[:, 0, :]
            .reshape(IC, 128, KERN).transpose(1, 0, 2)),
        conv_b=np.ascontiguousarray(np.asarray(
            inputs["L0_attn_conv_b"], dtype=np.float32).reshape(IC, 128).T),
        wbT=tiled_w(inputs["L0_attn_Wb"], IC, 4),
        wcT=tiled_w(inputs["L0_attn_Wc"], IC, 4),
        wdtT=T(inputs["L0_attn_Wdt"]),
        b_dt=np.asarray(inputs["L0_attn_b_dt"], dtype=np.float32),
        log_a=np.asarray(inputs["L0_attn_log_a"], dtype=np.float32),
        d_rep=np.repeat(np.asarray(inputs["L0_attn_d_param"], dtype=np.float32), HD),
        d_tab=np.ascontiguousarray(np.repeat(np.asarray(
            inputs["L0_attn_d_param"], dtype=np.float32), HD).reshape(IC, 128).T),
        gn_w=np.ascontiguousarray(np.asarray(
            inputs["L0_attn_gn_w"], dtype=np.float32).reshape(IC, 128).T),
        gn_b=np.ascontiguousarray(np.asarray(
            inputs["L0_attn_gn_b"], dtype=np.float32).reshape(IC, 128).T),
        woutT=T(inputs["L0_attn_Wout"]),
        w1T=np.ascontiguousarray(T(inputs["L0_ffn_W1"])),
        b1=np.ascontiguousarray(np.asarray(
            inputs["L0_ffn_b1"], dtype=np.float32).reshape(FC, 128).T),
        w2T=T(inputs["L0_ffn_W2"]),
        b2=np.asarray(inputs["L0_ffn_b2"], dtype=np.float32),
        identity=identity,
        trimask=trimask,
    )

    nc, out_names = build_program(b_loc=b_per, t_seq=x.shape[1])

    in_maps = []
    for c in range(NCHIP):
        m = dict(shared)
        m["x"] = np.ascontiguousarray(x[c * b_per:(c + 1) * b_per])
        in_maps.append(m)

    res = run_bass_kernel_spmd(nc, in_maps, core_ids=list(range(NCHIP)))
    outs = [r["out"] for r in res.results]
    return np.concatenate(outs, axis=0)


# version 23
# speedup vs baseline: 1.1154x; 1.1154x over previous
import contextlib

import numpy as np

import concourse.bass as bass
import concourse.mybir as mybir
from concourse.bass_utils import run_bass_kernel_spmd

F32 = mybir.dt.float32
BF16 = mybir.dt.bfloat16
AF = mybir.ActivationFunctionType
ALU = mybir.AluOpType
AX = mybir.AxisListType

DIM = 1024
INNER = 2048
HEADS = 8
STATE = 64
HD = INNER // HEADS
KERN = 4
FF = 4096
EPS = 1e-5
NCHIP = 8

DC = DIM // 128
IC = INNER // 128
FC = FF // 128
CH = 128


class Buf:
    __slots__ = ("name", "wr", "readers")

    def __init__(self, name=""):
        self.name = name
        self.wr = None
        self.readers = {}


class Em:

    ENGINES = ("tensor", "vector", "scalar", "sync", "gpsimd")

    def __init__(self, nc, ctx):
        self.nc = nc
        self.ctx = ctx
        self.sem = {}
        self.cnt = {}
        self.max_waited = {}
        self.qwait = {}
        self.waited = {e: {} for e in self.ENGINES}
        for e in self.ENGINES:
            self.sem[e] = ctx.enter_context(nc.semaphore("tl_" + e))
            self.cnt[e] = 0
        self.eng = {"tensor": nc.tensor, "vector": nc.vector, "scalar": nc.scalar,
                    "sync": nc.sync, "gpsimd": nc.gpsimd}

    def stream(self, name):
        self.sem[name] = self.ctx.enter_context(self.nc.semaphore("s_" + name))
        self.cnt[name] = 0
        return name

    def _wait(self, ename, need):
        w = self.waited[ename]
        eng = self.eng[ename]
        for s, v in need.items():
            if v <= 0 or w.get(s, 0) >= v:
                continue
            eng.wait_ge(self.sem[s], v)
            w[s] = v
            self.max_waited[s] = max(self.max_waited.get(s, 0), v)

    def _need(self, reads, writes):
        need = {}
        for b in reads:
            if b.wr:
                s, v = b.wr
                need[s] = max(need.get(s, 0), v)
        for b in writes:
            if b.wr:
                s, v = b.wr
                need[s] = max(need.get(s, 0), v)
            for s, v in b.readers.items():
                need[s] = max(need.get(s, 0), v)
        return need

    def op(self, ename, reads, writes, emit):
        self._wait(ename, self._need(reads, writes))
        ins = emit(self.eng[ename])
        self.cnt[ename] += 1
        ins.then_inc(self.sem[ename], 1)
        mark = (ename, self.cnt[ename])
        for b in writes:
            b.wr = mark
            b.readers = {}
        for b in reads:
            b.readers[ename] = self.cnt[ename]
        return ins

    def group(self, ename, reads, writes, emits):
        self._wait(ename, self._need(reads, writes))
        eng = self.eng[ename]
        ins = None
        for f in emits:
            ins = f(eng)
        self.cnt[ename] += 1
        ins.then_inc(self.sem[ename], 1)
        mark = (ename, self.cnt[ename])
        for b in writes:
            b.wr = mark
            b.readers = {}
        for b in reads:
            b.readers[ename] = self.cnt[ename]

    def dma(self, qname, sname, out_ap, in_ap, reads, writes):
        mw = self.cnt[sname]
        if mw > self.qwait.get(sname, 0):
            self.eng[qname].wait_ge(self.sem[sname], mw)
            self.qwait[sname] = mw
            self.waited[qname][sname] = max(self.waited[qname].get(sname, 0), mw)
        self._wait(qname, self._need(reads, writes))
        ins = self.eng[qname].dma_start(out=out_ap, in_=in_ap)
        self.cnt[sname] += 16
        ins.then_inc(self.sem[sname], 16)
        mark = (sname, self.cnt[sname])
        for b in writes:
            b.wr = mark
            b.readers = {}
        for b in reads:
            b.readers[sname] = self.cnt[sname]

    def barrier_wait(self, ename, sname, value):
        self._wait(ename, {sname: value})

    def fence(self):
        for e in self.ENGINES:
            need = {sn: v for sn, v in self.cnt.items() if v > 0 and sn != e}
            self._wait(e, need)


class Arena:

    def __init__(self, nc, words, dtype=F32, name="arena"):
        self.t = nc.alloc_sbuf_tensor(name, [128, words], dtype)
        self.words = words
        self.cursor = 0
        self.high = 0

    def alloc(self, shape):
        assert shape[0] == 128
        n = int(np.prod(shape[1:]))
        off = self.cursor
        assert off + n <= self.words, f"arena overflow: {off}+{n} > {self.words}"
        self.cursor = off + n
        self.high = max(self.high, self.cursor)
        ap = self.t[:, off:off + n]
        if len(shape) > 2:
            dims = " ".join(f"d{i}" for i in range(1, len(shape)))
            kw = {f"d{i}": s for i, s in enumerate(shape) if i > 0}
            ap = ap.rearrange(f"p ({dims}) -> p {dims}", **kw)
        return ap

    def reset(self, base):
        self.cursor = base

    def alloc_top(self, shape, off_from_top):
        n = int(np.prod(shape[1:]))
        off = self.words - off_from_top
        assert off >= 0 and off + n <= self.words
        ap = self.t[:, off:off + n]
        if len(shape) > 2:
            dims = " ".join(f"d{i}" for i in range(1, len(shape)))
            kw = {f"d{i}": sz for i, sz in enumerate(shape) if i > 0}
            ap = ap.rearrange(f"p ({dims}) -> p {dims}", **kw)
        return ap


def build_program(b_loc=2, t_seq=2048, debug=False):
    n_tile = t_seq // 512
    n_ch = t_seq // CH
    assert t_seq % 512 == 0

    nc = bass.Bass("TRN2", target_bir_lowering=False)
    ctx = contextlib.ExitStack()
    em = Em(nc, ctx)

    def din(name, shape):
        return nc.dram_tensor(name, shape, F32, kind="ExternalInput").ap()

    x_in = din("x", [b_loc, t_seq, DIM])
    wxT = din("wxT", [IC, 128, DC, 128])
    wzT = din("wzT", [IC, 128, DC, 128])
    conv_w = din("conv_w", [128, IC, KERN])
    conv_b = din("conv_b", [128, IC])
    wbT = din("wbT", [4, 128, IC, 128])
    wcT = din("wcT", [4, 128, IC, 128])
    wdtT = din("wdtT", [INNER, HEADS])
    b_dt_in = din("b_dt", [HEADS])
    log_a = din("log_a", [HEADS])
    d_rep = din("d_rep", [INNER])
    d_tab = din("d_tab", [128, IC])
    gn_w = din("gn_w", [128, IC])
    gn_b = din("gn_b", [128, IC])
    woutT = din("woutT", [INNER, DIM])
    w1T = din("w1T", [DIM, FF])
    b1_in = din("b1", [128, FC])
    w2T = din("w2T", [FF, DIM])
    b2_in = din("b2", [DIM])
    identity = din("identity", [128, 128])
    trimask = din("trimask", [128, 128])

    out = nc.dram_tensor("out", [b_loc, t_seq, DIM], F32, kind="ExternalOutput").ap()
    out_names = ["out"]

    yscr = nc.dram_tensor("yscr", [b_loc, n_ch, 128, IC, CH], F32, kind="Internal").ap()
    tw1 = nc.dram_tensor("tw1", [FC, 128, DC, 128], BF16, kind="Internal").ap()
    tw2 = nc.dram_tensor("tw2", [FF, DIM], BF16, kind="Internal").ap()

    dbg = {}
    if debug:
        def dout(name, shape):
            ap = nc.dram_tensor(name, shape, F32, kind="ExternalOutput").ap()
            out_names.append(name)
            dbg[name] = ap
            return ap
        dout("dbg_xc", [b_loc, n_tile, 128, IC, 512])
        dout("dbg_y", [b_loc, n_ch, 128, IC, CH])
        dout("dbg_stats", [b_loc, 2, 1, 16])
        dout("dbg_x2", [b_loc, t_seq, DIM])
        dout("dbg_dt", [b_loc, n_tile, 8, 512])
        dout("dbg_tern", [2, 128, 128])

    AR = Arena(nc, 46350)
    AR16 = Arena(nc, 13350, dtype=BF16, name="arena16")

    c_id = AR.alloc([128, 128]); b_id = Buf("id")
    c_tri = AR.alloc([128, 128]); b_tri = Buf("tri")
    c_convw = AR.alloc([128, IC, KERN]); b_convw = Buf("convw")
    c_convb = AR.alloc([128, IC]); b_convb = Buf("convb")
    c_gnw = AR.alloc([128, IC]); b_gnw = Buf("gnw")
    c_gnb = AR.alloc([128, IC]); b_gnb = Buf("gnb")
    c_scol = AR.alloc([128, b_loc, IC]); b_scol = Buf("scol")
    c_tcol = AR.alloc([128, b_loc, IC]); b_tcol = Buf("tcol")
    c_ssum = AR.alloc([128, b_loc, HEADS, n_ch]); b_ssum = Buf("ssum")
    c_ssq = AR.alloc([128, b_loc, HEADS, n_ch]); b_ssq = Buf("ssq")
    c_sred = AR.alloc([128, 2, HEADS]); b_sred = Buf("sred")
    c_state = AR.alloc([128, HEADS // 2, HD]); b_state = Buf("state")
    c_stash = AR.alloc([128, IC, KERN - 1]); b_stash = Buf("stash")
    c_onescol = AR.alloc([128, 1]); b_onescol = Buf("onescol")
    c_b1c = AR.alloc([128, FC]); b_b1c = Buf("b1c")
    c_keps = AR.alloc([128, 1]); b_keps = Buf("keps")
    c_ln1m = AR.alloc([128, 64]); b_ln1s = Buf("ln1s")
    c_ln1r = AR.alloc([128, 64])
    c_ones8 = AR.alloc([128, 128])
    b_ones8 = Buf("ones8")
    c_lnstat = AR.alloc([128, 16]); b_lnstat = Buf("lnstat")
    c_dtu = AR.alloc([128, 512]); b_dtu = Buf("dtu")
    c_sc8 = AR.alloc([128, 16]); b_sc8 = Buf("sc8")
    c_dcol = AR.alloc([128, 8]); b_dcol = Buf("dcol")
    c_rows = AR.alloc([128, 1540]); b_rows = Buf("rows")
    r_ones = c_rows[0:1, 0:512]
    r_b2 = c_rows[0:1, 512:512 + DIM]
    c_rows2 = AR.alloc([128, 66]); b_rows2 = Buf("rows2")
    r_work = c_rows2[0:1, 0:32]
    r_m16 = c_rows2[0:1, 32:48]
    r_r16 = c_rows2[0:1, 48:64]
    col_bdt = c_sc8[0:8, 0:1]
    col_a = c_sc8[0:8, 1:2]
    col_dtsum = c_sc8[0:8, 2:3]
    col_decay = c_sc8[0:8, 3:4]
    diag8 = c_sc8[0:8, 8:16]

    perm_end = AR.cursor
    _dbg_sizes = {"perm": perm_end}

    slab_lnT = AR.alloc_top([128, DC, 512], 4096)
    slab_xs = [AR.alloc_top([128, 1024], 6144 - 1024 * i) for i in range(2)]
    tn_buf = [AR.alloc_top([128, 1024], 8262 - 1024 * i) for i in range(2)]
    b_tnbuf = [Buf(f"tnbuf{i}") for i in range(2)]
    tn_acc = AR.alloc_top([128, 68], 6214); b_tnacc = Buf("tnacc")
    tn_th = AR.alloc_top([128, 2], 6146); b_tnth = Buf("tnth")
    tn_out = AR16.alloc([128, 1024]); b_tnout = Buf("tnout")
    P1_LIMIT = AR.words - 8262

    AR.reset(perm_end)
    c_dbc = AR.alloc([128, IC]); b_dbc = Buf("dbc")
    p1_wx = [AR.alloc([128, DC, 128]) for _ in range(2)]
    b_p1wx = [Buf(f"p1wx{i}") for i in range(2)]
    p1_wbc = [AR.alloc([128, IC, 128]) for _ in range(2)]
    b_p1wbc = [Buf(f"p1wbc{i}") for i in range(2)]
    p1_wdt = AR.alloc([128, IC, HEADS]); b_p1wdt = Buf("p1wdt")
    p1_xsub = [slab_xs[0], slab_xs[1]]
    b_p1xsub = [Buf(f"p1xsub{i}") for i in range(2)]
    p1_lnT = slab_lnT; b_p1lnT = Buf("p1lnT")
    p1_xc = AR.alloc([128, IC, 512]); b_p1xc = Buf("p1xc")
    p1_xpart = [AR.alloc([128, 512 + KERN - 1]) for _ in range(2)]
    b_p1xpart = [Buf(f"p1xpart{i}") for i in range(2)]
    p1_bc = AR.alloc([128, 8, 512]); b_p1bc = Buf("p1bc")
    p1_xcT = AR.alloc([128, IC, CH]); b_p1xcT = Buf("p1xcT")
    p1_xcd = AR.alloc([128, 2, CH]); b_p1xcd = Buf("p1xcd")
    p1_mp = AR.alloc([128, HEADS, CH]); b_p1mp = Buf("p1mp")
    p1_bchkT = AR.alloc([128, HEADS, STATE]); b_p1bchkT = Buf("p1bchkT")
    p1_cols = AR.alloc([128, 16]); b_p1cols = Buf("p1cols")
    p1_w = AR.alloc([128, 8]); b_p1w = Buf("p1w")
    p1_wi = AR.alloc([128, 8]); b_p1wi = Buf("p1wi")
    p1_dtw = AR.alloc([128, 8]); b_p1dtw = Buf("p1dtw")
    p1_dtw2 = AR.alloc([128, 8]); b_p1dtw2 = Buf("p1dtw2")
    p1_y = AR.alloc([128, IC, CH]); b_p1y = Buf("p1y")
    p1_yT = [AR.alloc([128, IC, CH]) for _ in range(1)] * 2
    b_p1yT = [Buf("p1yT0")] * 2
    p1_sq = AR.alloc([128, 2, CH]); b_p1sq = Buf("p1sq")

    AR.reset(perm_end)
    AR16.reset(1024)
    p2_wz = [AR.alloc([128, DC, 128]) for _ in range(2)]
    b_p2wz = [Buf(f"p2wz{i}") for i in range(2)]
    p2_wo = AR.alloc([128, IC, DIM]); b_p2wo = Buf("p2wo")
    p2_xsub = [AR.alloc([128, DIM]) for _ in range(4)]
    b_p2xsub = [Buf(f"p2xsub{i}") for i in range(4)]
    p2_lnsub = AR.alloc([128, DIM]); b_p2lnsub = Buf("p2lnsub")
    p2_lnT = slab_lnT; b_p2lnT = Buf("p2lnT")
    p2_z = AR.alloc([128, 512]); b_p2z = Buf("p2z")
    p2_yb = AR.alloc([128, IC, 512]); b_p2yb = Buf("p2yb")
    p2_x2t = [AR.alloc([128, DIM]) for _ in range(2)]
    b_p2x2t = [Buf(f"p2x2t{i}") for i in range(2)]
    p3_w1 = [AR16.alloc([128, DC, 128]) for _ in range(2)]
    b_p3w1 = [Buf(f"p3w1{i}") for i in range(2)]
    p3_w2 = [AR16.alloc([128, DIM]) for _ in range(2)]
    b_p3w2 = [Buf(f"p3w2{i}") for i in range(2)]
    p3_lnThi = AR16.alloc([128, DC, 256])
    p3_lnTlo = AR16.alloc([128, DC, 256])
    b_p3lnT = Buf("p3lnT")
    p3_h = [AR.alloc([128, 256]) for _ in range(2)]
    b_p3h = [Buf(f"p3h{i}") for i in range(2)]
    p3_hhi = [AR16.alloc([128, 256]) for _ in range(2)]
    p3_hlo = [AR16.alloc([128, 256]) for _ in range(2)]
    b_p3hhi = [Buf(f"p3hhi{i}") for i in range(2)]
    b_p3hlo = [Buf(f"p3hlo{i}") for i in range(2)]
    p3_x3 = AR.alloc([128, DIM]); b_p3x3 = Buf("p3x3")

    _dbg_sizes["max"] = AR.high
    assert AR.high <= AR.words - 4096, (AR.high, AR.words - 4096)
    PS = [nc.alloc_psum_tensor(f"ps{i}", [128, 512], F32) for i in range(8)]
    PB = [Buf(f"ps{i}") for i in range(8)]

    s_c = em.stream("consts")
    s_x = [em.stream(f"xin{i}") for i in range(2)]
    s_y = [em.stream(f"yout{i}") for i in range(2)]
    s_yr = [em.stream(f"yread{i}") for i in range(2)]
    s_wx = [em.stream(f"wx{i}") for i in range(2)]
    s_wbc = [em.stream(f"wbc{i}") for i in range(2)]
    s_wz = [em.stream(f"wz{i}") for i in range(2)]
    s_w1 = [em.stream(f"w1{i}") for i in range(2)]
    s_w2 = [em.stream(f"w2{i}") for i in range(2)]
    s_tn = em.stream("tnload")
    s_tw = em.stream("twrite")
    s_o = em.stream("oout")
    s_dbg = em.stream("dbgout")

    b_yscr = Buf("yscr")
    b_tw = Buf("twscr")

    prog = []

    def run(f):
        prog.append(f)

    def emit_consts():
        q, s = "sync", s_c
        em.dma(q, s, c_id, identity, [], [b_id])
        em.dma(q, s, c_tri, trimask, [], [b_tri])
        em.dma(q, s, c_convw, conv_w, [], [b_convw])
        em.dma(q, s, c_convb, conv_b, [], [b_convb])
        em.dma(q, s, c_gnw, gn_w, [], [b_gnw])
        em.dma(q, s, c_gnb, gn_b, [], [b_gnb])
        em.dma(q, s, c_b1c, b1_in, [], [b_b1c])
        em.dma(q, s, r_b2, b2_in.unsqueeze(0), [], [b_rows])
        em.dma(q, s, col_bdt, b_dt_in.unsqueeze(1), [], [b_sc8])
        em.dma(q, s, col_a, log_a.unsqueeze(1), [], [b_sc8])
        em.op("vector", [], [b_rows], lambda e: e.memset(r_ones, 1.0))
        em.op("vector", [], [b_onescol], lambda e: e.memset(c_onescol, 1.0))
        em.op("vector", [], [b_keps], lambda e: e.memset(c_keps, EPS))
        em.op("vector", [], [b_ones8], lambda e: e.memset(c_ones8[0:8, :], 1.0))
        em.op("vector", [], [b_ssum], lambda e: e.memset(c_ssum, 0.0))
        em.op("vector", [], [b_ssq], lambda e: e.memset(c_ssq, 0.0))
        em.op("scalar", [b_sc8], [b_sc8],
              lambda e: e.activation(col_a, col_a, AF.Exp))
        em.op("scalar", [b_sc8], [b_sc8],
              lambda e: e.mul(col_a, col_a, -1.0))
    run(emit_consts)

    s_tn = [em.stream(f"tn{i}") for i in range(2)]
    s_tw2 = em.stream("twrite2")
    tern_steps = []

    def build_tern_steps():
        w1v = w1T.rearrange("(a p) c -> p a c", p=128)
        w2v = w2T.rearrange("(a p) c -> p a c", p=128)

        def w1_ap(k, buf):
            return (buf.rearrange("p (a c) -> p a c", a=DC),
                    w1v[:, :, k * 128:(k + 1) * 128])

        def w2_ap(k, buf):
            return buf[:, 0:1024], w2v[:, k, :]

        def red_step(k, chunk_ap, col):
            def f():
                buf, bb = tn_buf[k % 2], b_tnbuf[k % 2]
                dst, src = chunk_ap(k, buf)
                em.dma("gpsimd", s_tn[k % 2], dst, src, [], [bb])
                em.op("vector", [bb], [b_tnacc],
                      lambda e: e.reduce_sum(tn_acc[:, col:col + 1], buf,
                                             axis=AX.X, apply_absolute_value=True))
            return f

        def fin_step(c0, c1, nelem):
            def f():
                em.op("vector", [b_tnacc], [b_tnacc],
                      lambda e: e.reduce_sum(tn_acc[:, 67:68], tn_acc[:, c0:c1],
                                             axis=AX.X))
                em.op("tensor", [b_tnacc, b_onescol], [PB[7]],
                      lambda e: e.matmul(PS[7][0:1, 16:17], tn_acc[:, 67:68],
                                         c_onescol, start=True, stop=True))
                th_row = c_rows2[0:1, 64:66]
                em.op("scalar", [PB[7]], [b_rows2],
                      lambda e: e.activation(th_row[0:1, 0:1], PS[7][0:1, 16:17],
                                             AF.Copy, scale=1.0 / nelem))
                em.op("vector", [b_rows2], [b_rows2],
                      lambda e: e.tensor_scalar_max(th_row[0:1, 0:1],
                                                    th_row[0:1, 0:1], 1e-8))
                em.op("scalar", [b_rows2], [b_rows2],
                      lambda e: e.mul(th_row[0:1, 0:1], th_row[0:1, 0:1], 0.5))
                em.op("scalar", [b_rows2], [b_rows2],
                      lambda e: e.mul(th_row[0:1, 1:2], th_row[0:1, 0:1], -1.0))
                em.op("tensor", [b_rows2, b_rows], [PB[7]],
                      lambda e: e.matmul(PS[7][:, 16:18], r_ones[0:1, 0:128],
                                         th_row, start=True, stop=True))
                em.op("scalar", [PB[7]], [b_tnth],
                      lambda e: e.copy(tn_th, PS[7][:, 16:18]))
            return f

        def tern_step(k, chunk_ap, out_fn):
            def f():
                buf, bb = tn_buf[k % 2], b_tnbuf[k % 2]
                dst, src = chunk_ap(k, buf)
                em.dma("gpsimd", s_tn[k % 2], dst, src, [], [bb])
                em.op("vector", [bb, b_tnth], [b_tnout],
                      lambda e: e.tensor_single_scalar(
                          tn_out, buf, tn_th[:, 1:2], ALU.is_lt))
                em.op("vector", [bb, b_tnth, b_tnout], [b_tnout],
                      lambda e: e.scalar_tensor_tensor(
                          tn_out, buf, tn_th[:, 0:1], tn_out, ALU.is_gt,
                          ALU.subtract))
                em.dma("gpsimd", s_tw2, *out_fn(k), [b_tnout], [b_tw])
            return f

        def w1_out(k):
            return (tw1[k], tn_out.rearrange("p (a c) -> p a c", a=DC))

        def w2_out(k):
            return (tw2.rearrange("(a p) c -> p a c", p=128)[:, k, :],
                    tn_out[:, 0:1024])

        for k in range(FC):
            tern_steps.append(red_step(k, w1_ap, k))
        tern_steps.append(fin_step(0, 32, DIM * FF))
        for k in range(FC):
            tern_steps.append(tern_step(k, w1_ap, w1_out))
        for k in range(FF // 128):
            tern_steps.append(red_step(k, w2_ap, 32 + k))
        tern_steps.append(fin_step(32, 64, FF * DIM))
        for k in range(FF // 128):
            tern_steps.append(tern_step(k, w2_ap, w2_out))

    build_tern_steps()

    def emit_tern_steps(n):
        for _ in range(n):
            if tern_steps:
                tern_steps.pop(0)()

    def emit_ln(src_ap, b_src, dst_ap, b_dst):
        em.op("vector", [b_src], [b_lnstat],
              lambda e: e.bn_stats(c_lnstat[:, 0:6], src_ap[:, 0:512]))
        em.op("vector", [b_src, b_lnstat], [b_lnstat],
              lambda e: e.bn_stats(c_lnstat[:, 6:12], src_ap[:, 512:1024]))
        em.op("vector", [b_lnstat], [b_lnstat],
              lambda e: e.bn_aggr(c_lnstat[:, 12:14], c_lnstat[:, 0:12]))
        em.op("scalar", [b_lnstat, b_keps], [b_lnstat],
              lambda e: e.activation(c_lnstat[:, 14:15], c_lnstat[:, 13:14], AF.Ln,
                                     bias=c_keps))
        em.op("scalar", [b_lnstat], [b_lnstat],
              lambda e: e.activation(c_lnstat[:, 14:15], c_lnstat[:, 14:15], AF.Exp,
                                     scale=-0.5))
        em.op("vector", [b_src, b_lnstat], [b_dst],
              lambda e: e.tensor_scalar(dst_ap, src_ap, c_lnstat[:, 12:13],
                                        c_lnstat[:, 14:15], ALU.subtract, ALU.mult))

    def save_ln_stats(mcol, rcol, bsave):
        em.op("vector", [b_lnstat], [bsave],
              lambda e: e.tensor_copy(mcol, c_lnstat[:, 12:13]))
        em.op("vector", [b_lnstat], [bsave],
              lambda e: e.tensor_copy(rcol, c_lnstat[:, 14:15]))

    def emit_transposes(srcs, b_srcs, dst_fn, bank0=6, evac="scalar"):
        n = len(srcs)
        done = 0
        while done < n:
            take = min(4, n - done)
            bi = bank0 + (done // 4) % 2
            pb, bb = PS[bi], PB[bi]
            fs = []
            rd = list(dict.fromkeys(b_srcs[done:done + take]))
            for i in range(take):
                src = srcs[done + i]
                fs.append(lambda e, src=src, i=i: e.transpose(
                    pb[:, i * 128:(i + 1) * 128], src, c_id))
            em.group("tensor", rd + [b_id], [bb], fs)
            dst, bdst = dst_fn(done, take)
            em.op(evac, [bb], [bdst],
                  lambda e, pb=pb, dst=dst, take=take: e.copy(
                      dst, pb[:, 0:take * 128].rearrange("p (a b) -> p a b", a=take)))
            done += take

    def p1_load_wdt():
        em.dma("sync", s_c, p1_wdt,
               wdtT.rearrange("(ic p) o -> p ic o", p=128), [], [b_p1wdt])
        em.dma("sync", s_c, c_dbc, d_tab, [], [b_dbc])
    run(p1_load_wdt)

    def pass1_tile(b, ti):
        t0 = ti * 512
        for sub in range(4):
            xb, bxb = p1_xsub[sub % 2], b_p1xsub[sub % 2]
            em.dma("sync", s_x[sub % 2], xb,
                   x_in[b, t0 + sub * 128:t0 + (sub + 1) * 128, :], [], [bxb])
            emit_ln(xb, bxb, xb, bxb)
            g = (b * n_tile + ti) * 4 + sub
            save_ln_stats(c_ln1m[:, g:g + 1], c_ln1r[:, g:g + 1], b_ln1s)
            srcs = [xb[:, dc * 128:(dc + 1) * 128] for dc in range(DC)]

            def dst_fn(done, take, sub=sub):
                return p1_lnT[:, done:done + take, sub * 128:(sub + 1) * 128], b_p1lnT
            emit_transposes(srcs, [bxb] * DC, dst_fn)

        for ic in range(IC):
            wxb, bwx = p1_wx[ic % 2], b_p1wx[ic % 2]
            em.dma("sync", s_wx[ic % 2], wxb, wxT[ic], [], [bwx])
            pb, bb = PS[ic % 2], PB[ic % 2]
            fs = []
            for dc in range(DC):
                fs.append(lambda e, dc=dc, pb=pb, wxb=wxb: e.matmul(
                    pb[:, 0:512], wxb[:, dc, :], p1_lnT[:, dc, :],
                    start=(dc == 0), stop=(dc == DC - 1)))
            em.group("tensor", [bwx, b_p1lnT], [bb], fs)
            xp, bxp = p1_xpart[ic % 2], b_p1xpart[ic % 2]
            em.op("scalar", [bb], [bxp],
                  lambda e, pb=pb, xp=xp: e.copy(xp[:, KERN - 1:], pb[:, 0:512]))
            em.op("vector", [b_stash], [bxp],
                  lambda e, xp=xp, ic=ic: e.tensor_copy(xp[:, 0:KERN - 1],
                                                        c_stash[:, ic, :]))
            em.op("vector", [bxp, b_convw], [b_p1xc],
                  lambda e, xp=xp, ic=ic: e.tensor_scalar_mul(
                      p1_xc[:, ic, :], xp[:, 0:512], c_convw[:, ic, 0:1]))
            for k in range(1, KERN):
                em.op("vector", [bxp, b_convw, b_p1xc], [b_p1xc],
                      lambda e, xp=xp, ic=ic, k=k: e.scalar_tensor_tensor(
                          p1_xc[:, ic, :], xp[:, k:k + 512], c_convw[:, ic, k:k + 1],
                          p1_xc[:, ic, :], ALU.mult, ALU.add))
            em.op("vector", [bxp], [b_stash],
                  lambda e, xp=xp, ic=ic: e.tensor_copy(c_stash[:, ic, :],
                                                        xp[:, 512:512 + KERN - 1]))
            em.op("scalar", [b_p1xc, b_convb], [b_p1xc],
                  lambda e, ic=ic: e.activation(p1_xc[:, ic, :], p1_xc[:, ic, :],
                                                AF.Silu, bias=c_convb[:, ic:ic + 1]))
            emit_tern_steps(1)
        if debug:
            em.dma("gpsimd", s_dbg, dbg["dbg_xc"][b, ti], p1_xc, [b_p1xc], [])

        for oc in range(8):
            wt = wbT if oc < 4 else wcT
            col = (oc % 4) * 128
            wb_, bw_ = p1_wbc[oc % 2], b_p1wbc[oc % 2]
            em.dma("sync", s_wbc[oc % 2], wb_, wt[oc % 4], [], [bw_])
            pb, bb = PS[2 + oc % 2], PB[2 + oc % 2]
            fs = []
            for ick in range(IC):
                fs.append(lambda e, ick=ick, wb_=wb_, pb=pb: e.matmul(
                    pb[:, 0:512], wb_[:, ick, :], p1_xc[:, ick, :],
                    start=(ick == 0), stop=(ick == IC - 1)))
            em.group("tensor", [bw_, b_p1xc], [bb], fs)
            em.op("scalar", [bb], [b_p1bc],
                  lambda e, pb=pb, oc=oc: e.copy(p1_bc[:, oc, :], pb[:, 0:512]))
            emit_tern_steps(1)
        pb4, bb4 = PS[4], PB[4]
        fs = []
        for ick in range(IC):
            fs.append(lambda e, ick=ick: e.matmul(
                pb4[0:8, 0:512], p1_wdt[:, ick, :], p1_xc[:, ick, :],
                start=(ick == 0), stop=(ick == IC - 1)))
        em.group("tensor", [b_p1wdt, b_p1xc], [bb4], fs)
        em.op("scalar", [bb4, b_sc8], [b_dtu],
              lambda e: e.activation(c_dtu[0:8, :], pb4[0:8, 0:512], AF.Exp,
                                     bias=col_bdt))
        em.op("scalar", [b_dtu], [b_dtu],
              lambda e: e.activation(c_dtu[0:8, :], c_dtu[0:8, :], AF.Ln, bias=1.0))
        if debug:
            em.dma("gpsimd", s_dbg, dbg["dbg_dt"][b, ti], c_dtu[0:8, :], [b_dtu], [])
        em.op("vector", [b_dtu, b_sc8], [b_dtu],
              lambda e: e.tensor_scalar_mul(c_dtu[32:40, :], c_dtu[0:8, :], col_a))
        for q in range(4):
            sl = slice(q * CH, (q + 1) * CH)
            em.op("vector", [b_dtu], [b_dtu],
                  lambda e, sl=sl: e.tensor_tensor_scan(
                      c_dtu[64:72, sl], c_dtu[32:40, sl], c_dtu[32:40, sl], 0.0,
                      ALU.add, ALU.bypass))

        for q in range(4):
            ch_idx = ti * 4 + q
            sl = slice(q * CH, (q + 1) * CH)
            pb6, bb6 = PS[6], PB[6]
            em.group("tensor", [b_dtu, b_id], [bb6], [
                lambda e: e.transpose(pb6[:, 0:8], c_dtu[0:8, sl], c_id[0:8, 0:8]),
                lambda e: e.transpose(pb6[:, 8:16], c_dtu[64:72, sl],
                                      c_id[64:72, 64:72]),
            ])
            em.op("scalar", [bb6], [b_p1cols],
                  lambda e: e.copy(p1_cols, pb6[:, 0:16]))
            em.op("scalar", [b_p1cols], [b_p1w],
                  lambda e: e.activation(p1_w, p1_cols[:, 8:16], AF.Exp))
            em.op("scalar", [b_p1cols], [b_p1wi],
                  lambda e: e.activation(p1_wi, p1_cols[:, 8:16], AF.Exp, scale=-1.0))
            em.op("vector", [b_p1cols, b_p1wi], [b_p1dtw],
                  lambda e: e.tensor_mul(p1_dtw, p1_cols[:, 0:8], p1_wi))
            em.op("vector", [b_dtu], [b_sc8],
                  lambda e, sl=sl: e.reduce_sum(col_dtsum, c_dtu[0:8, sl], axis=AX.X))
            em.op("scalar", [b_sc8], [b_sc8],
                  lambda e: e.activation(col_decay, col_dtsum, AF.Exp, scale=col_a))
            em.op("vector", [b_sc8, b_id], [b_sc8],
                  lambda e: e.tensor_scalar_mul(diag8, c_id[0:8, 0:8], col_decay))
            em.op("tensor", [b_sc8, b_ones8], [PB[7]],
                  lambda e: e.matmul(PS[7][:, 0:8], c_ones8[0:8, :], diag8,
                                     start=True, stop=True))
            em.op("scalar", [PB[7]], [b_dcol],
                  lambda e: e.copy(c_dcol, PS[7][:, 0:8]))
            em.op("vector", [b_p1dtw, b_dcol], [b_p1dtw2],
                  lambda e: e.tensor_mul(p1_dtw2, p1_dtw, c_dcol))
            srcs = [p1_xc[:, ic, sl] for ic in range(IC)]

            def dst_fn(done, take):
                return p1_xcT[:, done:done + take, :], b_p1xcT
            emit_transposes(srcs, [b_p1xc] * IC, dst_fn)
            for oc in range(4):
                pbb_, bbb_ = PS[6 + oc % 2], PB[6 + oc % 2]
                em.op("tensor", [b_p1bc, b_id], [bbb_],
                      lambda e, oc=oc, pbb_=pbb_: e.transpose(
                          pbb_[:, 0:128], p1_bc[:, oc, sl], c_id))
                for par in range(2):
                    h = oc * 2 + par
                    em.op("vector", [bbb_, b_p1dtw2], [b_p1bchkT],
                          lambda e, h=h, par=par, pbb_=pbb_: e.tensor_scalar_mul(
                              p1_bchkT[:, h, :], pbb_[:, par * 64:(par + 1) * 64],
                              p1_dtw2[:, h:h + 1]))
            for h in range(8):
                pbg, bbg = PS[2 + h % 2], PB[2 + h % 2]
                par = h % 2
                em.op("tensor", [b_p1bc], [bbg],
                      lambda e, h=h, par=par, pbg=pbg: e.matmul(
                          pbg[:, 0:128], p1_bc[par * 64:(par + 1) * 64, h // 2, sl],
                          p1_bc[par * 64:(par + 1) * 64, 4 + h // 2, sl],
                          start=True, stop=True))
                em.op("vector", [bbg, b_p1dtw, b_tri], [b_p1mp],
                      lambda e, h=h, pbg=pbg: e.scalar_tensor_tensor(
                          p1_mp[:, h, :], pbg[:, 0:128], p1_dtw[:, h:h + 1],
                          c_tri, ALU.mult, ALU.mult))
            for h in range(8):
                par = h % 2
                pby, bby = PS[2 + h % 2], PB[2 + h % 2]
                em.group("tensor", [b_p1mp, b_p1xcT, b_p1bc, b_state], [bby], [
                    lambda e, h=h, pby=pby: e.matmul(
                        pby[:, 128:384], p1_mp[:, h, :],
                        p1_xcT[:, 2 * h:2 * h + 2, :], start=True, stop=False),
                    lambda e, h=h, par=par, pby=pby: e.matmul(
                        pby[:, 128:384],
                        p1_bc[par * 64:(par + 1) * 64, 4 + h // 2, sl],
                        c_state[par * 64:(par + 1) * 64, h // 2, :],
                        start=False, stop=True),
                ])
                pbd, bbd = PS[4 + h % 2], PB[4 + h % 2]
                em.op("tensor", [b_p1bchkT, b_p1xcT], [bbd],
                      lambda e, h=h, par=par, pbd=pbd: e.matmul(
                          pbd[par * 64:(par + 1) * 64, 0:256], p1_bchkT[:, h, :],
                          p1_xcT[:, 2 * h:2 * h + 2, :], start=True, stop=True,
                          tile_position=(0, par * 64)))
                em.op("vector", [b_p1xcT, b_dbc], [b_p1xcd],
                      lambda e, h=h: e.tensor_mul(
                          p1_xcd, p1_xcT[:, 2 * h:2 * h + 2, :],
                          c_dbc[:, 2 * h:2 * h + 2].unsqueeze(2)
                          .broadcast_to([128, 2, CH])))
                em.op("vector", [bby, b_p1w, b_p1xcd], [b_p1y, b_ssum],
                      lambda e, h=h, b=b, ch_idx=ch_idx, pby=pby:
                      e.scalar_tensor_tensor(
                          p1_y[:, 2 * h:2 * h + 2, :],
                          pby[:, 128:384].rearrange("p (a c) -> p a c", a=2),
                          p1_w[:, h:h + 1], p1_xcd,
                          ALU.mult, ALU.add,
                          accum_out=c_ssum[:, b, h, ch_idx:ch_idx + 1]))
                em.op("scalar", [b_p1y], [b_p1sq, b_ssq],
                      lambda e, h=h, b=b, ch_idx=ch_idx: e.activation(
                          p1_sq, p1_y[:, 2 * h:2 * h + 2, :], AF.Square,
                          accum_out=c_ssq[:, b, h, ch_idx:ch_idx + 1]))
                em.op("vector", [bbd, b_dcol, b_state], [b_state],
                      lambda e, h=h, par=par, pbd=pbd: e.scalar_tensor_tensor(
                          c_state[par * 64:(par + 1) * 64, h // 2, :],
                          c_state[par * 64:(par + 1) * 64, h // 2, :],
                          c_dcol[par * 64:(par + 1) * 64, h:h + 1],
                          pbd[par * 64:(par + 1) * 64, 0:256],
                          ALU.mult, ALU.add))
            ybuf, bybuf = p1_yT[q % 2], b_p1yT[q % 2]
            srcs = [p1_y[:, k, :] for k in range(IC)]

            def dst_fn2(done, take, ybuf=ybuf, bybuf=bybuf):
                return ybuf[:, done:done + take, :], bybuf
            emit_transposes(srcs, [b_p1y] * IC, dst_fn2, bank0=6)
            em.dma("gpsimd", s_y[q % 2], yscr[b, ch_idx], ybuf, [bybuf], [b_yscr])
            if debug:
                em.dma("gpsimd", s_dbg, dbg["dbg_y"][b, ch_idx], ybuf, [bybuf], [])

    def pass1_seq(b):
        em.op("vector", [], [b_state], lambda e: e.memset(c_state, 0.0))
        em.op("vector", [], [b_stash], lambda e: e.memset(c_stash, 0.0))
        for ti in range(n_tile):
            pass1_tile(b, ti)

    for b in range(b_loc):
        run(lambda b=b: pass1_seq(b))

    run(lambda: emit_tern_steps(len(tern_steps)))
    run(em.fence)

    def stats_final(b):
        cnt = float(t_seq) * HD
        em.op("vector", [b_ssum], [b_sred],
              lambda e: e.reduce_sum(c_sred[:, 0, :], c_ssum[:, b, :, :], axis=AX.X))
        em.op("vector", [b_ssq], [b_sred],
              lambda e: e.reduce_sum(c_sred[:, 1, :], c_ssq[:, b, :, :], axis=AX.X))
        pb, bb = PS[0], PB[0]
        em.group("tensor", [b_sred, b_onescol], [bb], [
            lambda e: e.matmul(pb[0:1, 0:8], c_onescol, c_sred[:, 0, :],
                               start=True, stop=True),
            lambda e: e.matmul(pb[0:1, 8:16], c_onescol, c_sred[:, 1, :],
                               start=True, stop=True),
        ])
        em.op("scalar", [bb], [b_rows2],
              lambda e: e.activation(r_work[0:1, 0:8], pb[0:1, 0:8], AF.Copy,
                                     scale=1.0 / cnt))
        em.op("scalar", [bb], [b_rows2],
              lambda e: e.activation(r_work[0:1, 8:16], pb[0:1, 8:16], AF.Copy,
                                     scale=1.0 / cnt))
        em.op("vector", [b_rows2], [b_rows2],
              lambda e: e.tensor_mul(r_work[0:1, 16:24], r_work[0:1, 0:8],
                                     r_work[0:1, 0:8]))
        em.op("vector", [b_rows2], [b_rows2],
              lambda e: e.tensor_sub(r_work[0:1, 16:24], r_work[0:1, 8:16],
                                     r_work[0:1, 16:24]))
        em.op("scalar", [b_rows2, b_keps], [b_rows2],
              lambda e: e.activation(r_work[0:1, 24:32], r_work[0:1, 16:24],
                                     AF.Ln, bias=c_keps[0:1, 0:1]))
        em.op("scalar", [b_rows2], [b_rows2],
              lambda e: e.activation(r_work[0:1, 24:32], r_work[0:1, 24:32],
                                     AF.Exp, scale=-0.5))
        for (dst, src) in ((r_m16, r_work[0:1, 0:8]), (r_r16, r_work[0:1, 24:32])):
            em.op("vector", [b_rows2], [b_rows2],
                  lambda e, dst=dst, src=src: e.tensor_copy(dst[0:1, 0:16:2], src))
            em.op("vector", [b_rows2], [b_rows2],
                  lambda e, dst=dst, src=src: e.tensor_copy(dst[0:1, 1:16:2], src))
        if debug:
            em.dma("gpsimd", s_dbg, dbg["dbg_stats"][b, 0], r_m16, [b_rows2], [])
            em.dma("gpsimd", s_dbg, dbg["dbg_stats"][b, 1], r_r16, [b_rows2], [])
        pb1, bb1 = PS[1], PB[1]
        em.group("tensor", [b_rows2, b_rows], [bb1], [
            lambda e: e.matmul(pb1[:, 0:16], r_ones[0:1, 0:128], r_r16,
                               start=True, stop=True),
            lambda e: e.matmul(pb1[:, 16:32], r_ones[0:1, 0:128], r_m16,
                               start=True, stop=True),
        ])
        em.op("vector", [bb1, b_gnw], [b_scol],
              lambda e: e.tensor_mul(c_scol[:, b, :], c_gnw, pb1[:, 0:16]))
        em.op("vector", [bb1, b_scol], [b_tcol],
              lambda e: e.tensor_mul(c_tcol[:, b, :], c_scol[:, b, :],
                                     pb1[:, 16:32]))
        em.op("vector", [b_tcol, b_gnb], [b_tcol],
              lambda e: e.tensor_sub(c_tcol[:, b, :], c_gnb, c_tcol[:, b, :]))

    for b in range(b_loc):
        run(lambda b=b: stats_final(b))

    def p2a_load():
        em.dma("sync", s_c, p2_wo,
               woutT.rearrange("(ic p) d -> p ic d", p=128), [], [b_p2wo])
    run(p2a_load)

    p2a_fetched = set()

    def p2a_fetch(b, ti):
        if (b, ti) in p2a_fetched:
            return
        p2a_fetched.add((b, ti))
        t0 = ti * 512
        for sub in range(4):
            xb, bxb = p2_xsub[sub], b_p2xsub[sub]
            em.dma("sync", s_x[sub % 2], xb,
                   x_in[b, t0 + sub * 128:t0 + (sub + 1) * 128, :], [], [bxb])
        for q in range(4):
            em.dma("sync", s_yr[q % 2], p2_yb[:, :, q * 128:(q + 1) * 128],
                   yscr[b, ti * 4 + q], [b_yscr], [b_p2yb])

    def pass2_tile(b, ti, nxt):
        t0 = ti * 512
        p2a_fetch(b, ti)
        for sub in range(4):
            xb, bxb = p2_xsub[sub], b_p2xsub[sub]
            g = (b * n_tile + ti) * 4 + sub
            em.op("vector", [bxb, b_ln1s], [b_p2lnsub],
                  lambda e, xb=xb, g=g: e.tensor_scalar(
                      p2_lnsub, xb, c_ln1m[:, g:g + 1], c_ln1r[:, g:g + 1],
                      ALU.subtract, ALU.mult))
            srcs = [p2_lnsub[:, dc * 128:(dc + 1) * 128] for dc in range(DC)]

            def dst_fn(done, take, sub=sub):
                return p2_lnT[:, done:done + take, sub * 128:(sub + 1) * 128], b_p2lnT
            emit_transposes(srcs, [b_p2lnsub] * DC, dst_fn)
        for oc in range(IC):
            wzb, bwz = p2_wz[oc % 2], b_p2wz[oc % 2]
            em.dma("sync", s_wz[oc % 2], wzb, wzT[oc], [], [bwz])
            pb, bb = PS[oc % 2], PB[oc % 2]
            fs = []
            for dc in range(DC):
                fs.append(lambda e, dc=dc, pb=pb, wzb=wzb: e.matmul(
                    pb[:, 0:512], wzb[:, dc, :], p2_lnT[:, dc, :],
                    start=(dc == 0), stop=(dc == DC - 1)))
            em.group("tensor", [bwz, b_p2lnT], [bb], fs)
            em.op("scalar", [bb], [b_p2z],
                  lambda e, pb=pb: e.activation(p2_z, pb[:, 0:512], AF.Silu))
            em.op("vector", [b_p2yb, b_scol, b_tcol], [b_p2yb],
                  lambda e, oc=oc, b=b: e.tensor_scalar(
                      p2_yb[:, oc, :], p2_yb[:, oc, :], c_scol[:, b, oc:oc + 1],
                      c_tcol[:, b, oc:oc + 1], ALU.mult, ALU.add))
            em.op("vector", [b_p2yb, b_p2z], [b_p2yb],
                  lambda e, oc=oc: e.tensor_mul(p2_yb[:, oc, :], p2_yb[:, oc, :],
                                                p2_z))
        for half in range(2):
            for j in range(2):
                sub = half * 2 + j
                pba, pbb = PS[2 + 2 * j], PS[3 + 2 * j]
                bba, bbb = PB[2 + 2 * j], PB[3 + 2 * j]
                fs = []
                for ick in range(IC):
                    for hh in range(2):
                        pbx = pba if hh == 0 else pbb
                        fs.append(lambda e, ick=ick, hh=hh, sub=sub, pbx=pbx:
                                  e.matmul(pbx[:, 0:512],
                                           p2_yb[:, ick, sub * 128:(sub + 1) * 128],
                                           p2_wo[:, ick, hh * 512:(hh + 1) * 512],
                                           start=(ick == 0), stop=(ick == IC - 1)))
                em.group("tensor", [b_p2yb, b_p2wo], [bba, bbb], fs)
            for j in range(2):
                sub = half * 2 + j
                pba, pbb = PS[2 + 2 * j], PS[3 + 2 * j]
                bba, bbb = PB[2 + 2 * j], PB[3 + 2 * j]
                em.op("vector", [bba], [b_lnstat],
                      lambda e, pba=pba: e.bn_stats(c_lnstat[:, 0:6], pba[:, 0:512]))
                em.op("vector", [bbb, b_lnstat], [b_lnstat],
                      lambda e, pbb=pbb: e.bn_stats(c_lnstat[:, 6:12], pbb[:, 0:512]))
                em.op("vector", [b_lnstat], [b_lnstat],
                      lambda e: e.bn_aggr(c_lnstat[:, 12:14], c_lnstat[:, 0:12]))
                em.op("scalar", [b_lnstat, b_keps], [b_lnstat],
                      lambda e: e.activation(c_lnstat[:, 14:15], c_lnstat[:, 13:14],
                                             AF.Ln, bias=c_keps))
                em.op("scalar", [b_lnstat], [b_lnstat],
                      lambda e: e.activation(c_lnstat[:, 14:15], c_lnstat[:, 14:15],
                                             AF.Exp, scale=-0.5))
                x2b, bx2b = p2_x2t[j], b_p2x2t[j]
                for hh in range(2):
                    pbx, bbx = (pba, bba) if hh == 0 else (pbb, bbb)
                    em.op("vector", [bbx, b_lnstat], [b_p2lnsub],
                          lambda e, pbx=pbx, hh=hh: e.tensor_scalar(
                              p2_lnsub[:, hh * 512:(hh + 1) * 512], pbx[:, 0:512],
                              c_lnstat[:, 12:13], c_lnstat[:, 14:15],
                              ALU.subtract, ALU.mult))
                em.op("vector", [b_p2lnsub, b_p2xsub[sub], bx2b], [bx2b],
                      lambda e, x2b=x2b, sub=sub: e.tensor_add(
                          x2b, p2_lnsub, p2_xsub[sub]))
                em.op("vector", [bx2b], [b_lnstat],
                      lambda e, x2b=x2b: e.bn_stats(c_lnstat[:, 0:6],
                                                    x2b[:, 0:512]))
                em.op("vector", [bx2b, b_lnstat], [b_lnstat],
                      lambda e, x2b=x2b: e.bn_stats(c_lnstat[:, 6:12],
                                                    x2b[:, 512:1024]))
                em.op("vector", [b_lnstat], [b_lnstat],
                      lambda e: e.bn_aggr(c_lnstat[:, 12:14], c_lnstat[:, 0:12]))
                em.op("scalar", [b_lnstat, b_keps], [b_lnstat],
                      lambda e: e.activation(c_lnstat[:, 14:15], c_lnstat[:, 13:14],
                                             AF.Ln, bias=c_keps))
                em.op("scalar", [b_lnstat], [b_lnstat],
                      lambda e: e.activation(c_lnstat[:, 14:15], c_lnstat[:, 14:15],
                                             AF.Exp, scale=-0.5))
                em.op("vector", [bx2b, b_lnstat], [b_p2lnsub],
                      lambda e, x2b=x2b: e.tensor_scalar(
                          p2_lnsub, x2b, c_lnstat[:, 12:13], c_lnstat[:, 14:15],
                          ALU.subtract, ALU.mult))
                srcs = [p2_lnsub[:, dc * 128:(dc + 1) * 128] for dc in range(DC)]
                done = 0
                while done < DC:
                    take = min(4, DC - done)
                    bi = 6 + (done // 4) % 2
                    pb, bb = PS[bi], PB[bi]
                    fs = []
                    for i in range(take):
                        src = srcs[done + i]
                        fs.append(lambda e, src=src, i=i, pb=pb: e.transpose(
                            pb[:, i * 128:(i + 1) * 128], src, c_id))
                    em.group("tensor", [b_p2lnsub, b_id], [bb], fs)
                    hi = p3_lnThi[:, done:done + take, j * 128:(j + 1) * 128]
                    lo = p3_lnTlo[:, done:done + take, j * 128:(j + 1) * 128]
                    pv = pb[:, 0:take * 128].rearrange("p (a c) -> p a c", a=take)
                    em.op("scalar", [bb], [b_p3lnT],
                          lambda e, hi=hi, pv=pv: e.copy(hi, pv))
                    em.op("vector", [bb, b_p3lnT], [b_p3lnT],
                          lambda e, lo=lo, pv=pv, hi=hi: e.tensor_sub(lo, pv, hi))
                    done += take
            def ld_w(fc):
                w1b, bw1 = p3_w1[fc % 2], b_p3w1[fc % 2]
                w2b, bw2 = p3_w2[fc % 2], b_p3w2[fc % 2]
                em.dma("sync", s_w1[fc % 2], w1b, tw1[fc], [b_tw], [bw1])
                em.dma("sync", s_w2[fc % 2], w2b, tw2[fc * 128:(fc + 1) * 128, :],
                       [b_tw], [bw2])

            def b2_init():
                for j in range(2):
                    pba, pbb = PS[2 + 2 * j], PS[3 + 2 * j]
                    bba, bbb = PB[2 + 2 * j], PB[3 + 2 * j]
                    em.group("tensor", [b_rows], [bba, bbb], [
                        lambda e, pba=pba: e.matmul(
                            pba[:, 0:512], r_ones[0:1, 0:128], r_b2[0:1, 0:512],
                            start=True, stop=False),
                        lambda e, pbb=pbb: e.matmul(
                            pbb[:, 0:512], r_ones[0:1, 0:128], r_b2[0:1, 512:1024],
                            start=True, stop=False),
                    ])

            def h_stage(fc):
                w1b, bw1 = p3_w1[fc % 2], b_p3w1[fc % 2]
                pbh, bbh = PS[fc % 2], PB[fc % 2]
                fs = []
                for dc in range(DC):
                    for part in range(2):
                        lt = p3_lnThi if part == 0 else p3_lnTlo
                        fs.append(lambda e, dc=dc, pbh=pbh, w1b=w1b, lt=lt,
                                  first=(dc == 0 and part == 0),
                                  last=(dc == DC - 1 and part == 1): e.matmul(
                            pbh[:, 0:256], w1b[:, dc, :], lt[:, dc, :],
                            start=first, stop=last))
                em.group("tensor", [bw1, b_p3lnT], [bbh], fs)
                hb, bhb = p3_h[fc % 2], b_p3h[fc % 2]
                hhi, hlo = p3_hhi[fc % 2], p3_hlo[fc % 2]
                bhhi, bhlo = b_p3hhi[fc % 2], b_p3hlo[fc % 2]
                em.op("scalar", [bbh, b_b1c], [bhb],
                      lambda e, pbh=pbh, hb=hb, fc=fc: e.activation(
                          hb, pbh[:, 0:256], AF.Gelu, bias=c_b1c[:, fc:fc + 1]))
                em.op("scalar", [bhb], [bhhi],
                      lambda e, hb=hb, hhi=hhi: e.copy(hhi, hb))
                em.op("vector", [bhb, bhhi], [bhlo],
                      lambda e, hb=hb, hhi=hhi, hlo=hlo: e.tensor_sub(hlo, hb, hhi))

            def w2_stage(fc):
                hhi, hlo = p3_hhi[fc % 2], p3_hlo[fc % 2]
                bhhi, bhlo = b_p3hhi[fc % 2], b_p3hlo[fc % 2]
                w2b, bw2 = p3_w2[fc % 2], b_p3w2[fc % 2]
                last = fc == FC - 1
                for j in range(2):
                    pba, pbb = PS[2 + 2 * j], PS[3 + 2 * j]
                    bba, bbb = PB[2 + 2 * j], PB[3 + 2 * j]
                    fs = []
                    for part in range(2):
                        hp = hhi if part == 0 else hlo
                        for hh in range(2):
                            pbx = pba if hh == 0 else pbb
                            fs.append(lambda e, pbx=pbx, hp=hp, j=j, w2b=w2b,
                                      hh=hh, lastm=(last and part == 1): e.matmul(
                                pbx[:, 0:512], hp[:, j * 128:(j + 1) * 128],
                                w2b[:, hh * 512:(hh + 1) * 512],
                                start=False, stop=lastm))
                    em.group("tensor", [bhhi, bhlo, bw2], [bba, bbb], fs)

            ld_w(0)
            for fc in range(FC):
                h_stage(fc)
                if fc == 0:
                    b2_init()
                if fc > 0:
                    w2_stage(fc - 1)
                if fc + 1 < FC:
                    ld_w(fc + 1)
            w2_stage(FC - 1)
            for j in range(2):
                sub = half * 2 + j
                pba, pbb = PS[2 + 2 * j], PS[3 + 2 * j]
                bba, bbb = PB[2 + 2 * j], PB[3 + 2 * j]
                em.op("vector", [bba], [b_lnstat],
                      lambda e, pba=pba: e.bn_stats(c_lnstat[:, 0:6], pba[:, 0:512]))
                em.op("vector", [bbb, b_lnstat], [b_lnstat],
                      lambda e, pbb=pbb: e.bn_stats(c_lnstat[:, 6:12], pbb[:, 0:512]))
                em.op("vector", [b_lnstat], [b_lnstat],
                      lambda e: e.bn_aggr(c_lnstat[:, 12:14], c_lnstat[:, 0:12]))
                em.op("scalar", [b_lnstat, b_keps], [b_lnstat],
                      lambda e: e.activation(c_lnstat[:, 14:15], c_lnstat[:, 13:14],
                                             AF.Ln, bias=c_keps))
                em.op("scalar", [b_lnstat], [b_lnstat],
                      lambda e: e.activation(c_lnstat[:, 14:15], c_lnstat[:, 14:15],
                                             AF.Exp, scale=-0.5))
                for hh in range(2):
                    pbx, bbx = (pba, bba) if hh == 0 else (pbb, bbb)
                    em.op("vector", [bbx, b_lnstat], [b_p3x3],
                          lambda e, pbx=pbx, hh=hh: e.tensor_scalar(
                              p3_x3[:, hh * 512:(hh + 1) * 512], pbx[:, 0:512],
                              c_lnstat[:, 12:13], c_lnstat[:, 14:15],
                              ALU.subtract, ALU.mult))
                em.op("vector", [b_p3x3, b_p2x2t[j]], [b_p3x3],
                      lambda e, j=j: e.tensor_add(p3_x3, p3_x3, p2_x2t[j]))
                em.dma("gpsimd", s_o, out[b, t0 + sub * 128:t0 + (sub + 1) * 128, :],
                       p3_x3, [b_p3x3], [])
        if nxt is not None:
            p2a_fetch(*nxt)

    tiles_2 = [(b, ti) for b in range(b_loc) for ti in range(n_tile)]
    for i, (b, ti) in enumerate(tiles_2):
        nxt = tiles_2[i + 1] if i + 1 < len(tiles_2) else None
        run(lambda b=b, ti=ti, nxt=nxt: pass2_tile(b, ti, nxt))

    with nc.Block() as blk:
        @blk.sync
        def _(sync):
            for f in prog:
                f()
            em.barrier_wait("gpsimd", s_o, em.cnt[s_o])
            if debug:
                em.barrier_wait("gpsimd", s_dbg, em.cnt[s_dbg])

    ctx.close()
    return nc, out_names


def kernel(**inputs):
    x = np.ascontiguousarray(inputs["L0_attn_input_tensor"], dtype=np.float32)
    b_per = x.shape[0] // NCHIP

    def T(a):
        return np.ascontiguousarray(np.asarray(a, dtype=np.float32).T)

    identity = np.eye(128, dtype=np.float32)
    jj, tt = np.meshgrid(np.arange(128), np.arange(128), indexing="ij")
    trimask = (tt >= jj).astype(np.float32)

    def tiled_w(w, kchunks, mchunks):
        wt = T(w)
        K_, M_ = wt.shape
        return np.ascontiguousarray(
            wt.reshape(kchunks, 128, mchunks, 128).transpose(2, 1, 0, 3))

    shared = dict(
        wxT=tiled_w(inputs["L0_attn_Wx"], DC, IC),
        wzT=tiled_w(inputs["L0_attn_Wz"], DC, IC),
        conv_w=np.ascontiguousarray(np.asarray(
            inputs["L0_attn_conv_w"], dtype=np.float32)[:, 0, :]
            .reshape(IC, 128, KERN).transpose(1, 0, 2)),
        conv_b=np.ascontiguousarray(np.asarray(
            inputs["L0_attn_conv_b"], dtype=np.float32).reshape(IC, 128).T),
        wbT=tiled_w(inputs["L0_attn_Wb"], IC, 4),
        wcT=tiled_w(inputs["L0_attn_Wc"], IC, 4),
        wdtT=T(inputs["L0_attn_Wdt"]),
        b_dt=np.asarray(inputs["L0_attn_b_dt"], dtype=np.float32),
        log_a=np.asarray(inputs["L0_attn_log_a"], dtype=np.float32),
        d_rep=np.repeat(np.asarray(inputs["L0_attn_d_param"], dtype=np.float32), HD),
        d_tab=np.ascontiguousarray(np.repeat(np.asarray(
            inputs["L0_attn_d_param"], dtype=np.float32), HD).reshape(IC, 128).T),
        gn_w=np.ascontiguousarray(np.asarray(
            inputs["L0_attn_gn_w"], dtype=np.float32).reshape(IC, 128).T),
        gn_b=np.ascontiguousarray(np.asarray(
            inputs["L0_attn_gn_b"], dtype=np.float32).reshape(IC, 128).T),
        woutT=T(inputs["L0_attn_Wout"]),
        w1T=np.ascontiguousarray(T(inputs["L0_ffn_W1"])),
        b1=np.ascontiguousarray(np.asarray(
            inputs["L0_ffn_b1"], dtype=np.float32).reshape(FC, 128).T),
        w2T=T(inputs["L0_ffn_W2"]),
        b2=np.asarray(inputs["L0_ffn_b2"], dtype=np.float32),
        identity=identity,
        trimask=trimask,
    )

    nc, out_names = build_program(b_loc=b_per, t_seq=x.shape[1])

    in_maps = []
    for c in range(NCHIP):
        m = dict(shared)
        m["x"] = np.ascontiguousarray(x[c * b_per:(c + 1) * b_per])
        in_maps.append(m)

    res = run_bass_kernel_spmd(nc, in_maps, core_ids=list(range(NCHIP)))
    outs = [r["out"] for r in res.results]
    return np.concatenate(outs, axis=0)
